# version 02 (final)
# baseline (speedup 1.0000x reference)
import numpy as np
import ml_dtypes
import concourse.bass as bass
import concourse.mybir as mybir
from concourse.bass_utils import run_bass_kernel_spmd

F32 = mybir.dt.float32
BF16 = mybir.dt.bfloat16
AF = mybir.ActivationFunctionType
ALU = mybir.AluOpType
AX = mybir.AxisListType

NC = 8
T = 1024
TB = 512
D = 2048
KC = 16
S = 8192
BF = ml_dtypes.bfloat16

PI = float(np.float32(np.pi))
HALF_PI = float(np.float32(np.pi / 2))
TWO_PI_HI = float(np.float32(2 * np.pi))
TWO_PI_LO = float(np.float32(np.float64(2 * np.pi) - np.float64(np.float32(2 * np.pi))))
N_AS_ITERS = 8
PI_SAFE = float(np.float32(3.1415925))


def zig_tiles(c):
    return [8 * m + (c if m % 2 == 0 else 7 - c) for m in range(8)]


def core_rows(c):
    r = []
    for g in zig_tiles(c):
        r.extend(range(128 * g, 128 * g + 128))
    return np.array(r)


def pack_lhsT(wT):
    i, o = wT.shape
    kc, oc = i // 128, o // 128
    return np.ascontiguousarray(wT.reshape(kc, 128, oc, 128).transpose(2, 0, 1, 3))


def pack_rhs(wT, ob):
    i, o = wT.shape
    kc, nb = i // 128, o // ob
    return np.ascontiguousarray(wT.reshape(kc, 128, nb, ob).transpose(2, 0, 1, 3))


def fm(x_rows):
    return np.ascontiguousarray(x_rows.T)


def col16(v):
    return np.ascontiguousarray(v.reshape(-1, 128).T)


class Ev:
    __slots__ = ("sem", "val")

    def __init__(self, sem, val):
        self.sem = sem
        self.val = val


class Bld:
    def __init__(self, nc):
        self.nc = nc
        self.t = nc.tensor
        self.v = nc.vector
        self.s = nc.scalar
        self.g = nc.gpsimd
        self.y = nc.sync
        self.sems = {}
        self.cnt = {}
        self.ctx = []
        self.last_wait = {}
        for s_ in ("gdma", "ydma", "mm", "ve", "se", "gc", "gw0", "gw1", "gx0", "gx1"):
            h = nc.alloc_semaphore(name=s_)
            self.sems[s_] = h
            self.cnt[s_] = 0

    def mark(self, inst, sem, by=1):
        self.cnt[sem] += by
        inst.then_inc(self.sems[sem], by)
        return Ev(sem, self.cnt[sem])

    def dma(self, eng, out, in_, sem=None):
        if sem is None:
            sem = "gdma" if eng is self.g else "ydma"
        if self.cnt[sem] > 0:
            self.wait(eng, Ev(sem, self.cnt[sem]))
        return self.mark(eng.dma_start(out=out, in_=in_), sem, 16)

    def wait(self, eng, *evs):
        en = id(eng)
        flat = []
        for ev in evs:
            if isinstance(ev, (list, tuple)):
                flat.extend(ev)
            else:
                flat.append(ev)
        for ev in flat:
            if ev is None:
                continue
            val = ev.val
            if ev.sem in ("gdma", "ydma", "gw0", "gw1", "gx0", "gx1"):
                val = self.cnt[ev.sem]
            key = (en, ev.sem)
            if self.last_wait.get(key, -1) >= val:
                continue
            eng.wait_ge(self.sems[ev.sem], val)
            self.last_wait[key] = val

    def sb(self, name, shape, dtype):
        cm = self.nc.sbuf_tensor(name, list(shape), dtype)
        t = cm.__enter__()
        self.ctx.append(cm)
        return t

    def ps(self, name, shape, dtype=F32):
        cm = self.nc.psum_tensor(name, list(shape), dtype)
        t = cm.__enter__()
        self.ctx.append(cm)
        return t

    def close(self):
        for cm in reversed(self.ctx):
            cm.__exit__(None, None, None)

    def vts(self, out, in0, s1, s2, op0, op1=None, accum=None, deps=()):
        self.wait(self.v, *deps)
        if s2 is None and accum is None and op1 is None:
            i = self.v.tensor_scalar(out, in0, s1, None, op0)
        else:
            i = self.v.tensor_scalar(out, in0, s1, s2, op0,
                                     op1 if op1 is not None else ALU.add,
                                     accum_out=accum)
        return self.mark(i, "ve")

    def vtt(self, out, a, bb, op, deps=()):
        self.wait(self.v, *deps)
        return self.mark(self.v.tensor_tensor(out, a, bb, op), "ve")

    def vstt(self, out, in0, sc, in1, op0, op1, accum=None, deps=()):
        self.wait(self.v, *deps)
        return self.mark(self.v.scalar_tensor_tensor(out, in0, sc, in1, op0, op1,
                                                     accum_out=accum), "ve")

    def vcp(self, out, in_, deps=()):
        self.wait(self.v, *deps)
        return self.mark(self.v.tensor_copy(out, in_), "ve")

    def act(self, out, in_, func, bias=None, scale=None, accum=None, deps=()):
        self.wait(self.s, *deps)
        kw = {}
        if bias is not None:
            kw["bias"] = bias
        if scale is not None:
            kw["scale"] = scale
        if accum is not None:
            kw["accum_out"] = accum
        return self.mark(self.s.activation(out, in_, func, **kw), "se")

    def red(self, out, in_, op, deps=()):
        self.wait(self.v, *deps)
        return self.mark(self.v.tensor_reduce(out, in_, AX.X, op), "ve")

    def mm(self, out, lhsT, rhs, start, stop, deps=()):
        self.wait(self.t, *deps)
        i = self.t.matmul(out, lhsT, rhs, start=start, stop=stop)
        if stop:
            return self.mark(i, "mm")
        return None

    def tp(self, out, in_, ident, deps=(), markit=True):
        self.wait(self.t, *deps)
        i = self.t.transpose(out, in_, ident)
        if markit:
            return self.mark(i, "mm")
        return None


def make_ident(b, tile):
    e0 = b.mark(b.g.memset(tile, 0.0), "gc")
    b.wait(b.g, e0)
    inst = b.g.affine_select(out=tile, in_=tile, compare_op=ALU.not_equal,
                             fill=1.0, base=0, pattern=[[-1, tile.shape[0]]],
                             channel_multiplier=1)
    return b.mark(inst, "gc")



def stats_pass(b, nkc, width, get_chunk_bf, get_chunk_sq_src, ps_stat, sq_scr,
               ones_col, deps=()):
    ev = None
    ev_sq_prev = None
    for kc in range(nkc):
        sl, ev_sl = get_chunk_bf(kc)
        sq_src, ev_sqs = get_chunk_sq_src(kc)
        ev_sq = b.act(sq_scr[:, 0:width], sq_src, AF.Square,
                      deps=(ev_sqs, ev_sq_prev))
        b.wait(b.t, ev_sl, ev_sq, *deps)
        for tb0 in range(0, width, TB):
            b.mm(ps_stat[0:1, tb0:tb0 + TB], ones_col, sl[:, tb0:tb0 + TB],
                 start=(kc == 0), stop=False)
            e = b.t.matmul(ps_stat[0:1, width + tb0:width + tb0 + TB], ones_col,
                           sq_scr[:, tb0:tb0 + TB], start=(kc == 0),
                           stop=(kc == nkc - 1))
            if kc == nkc - 1 and tb0 + TB >= width:
                ev = b.mark(e, "mm")
        ev_sq_prev = b.mark(b.t.nop(), "mm") if False else ev_sq
        ev_sq_prev = b.mark(b.t.nop(), "mm")
    return ev


def row_stats(b, ps_stat, width, n, eps, rt, ev_stat, sub_mean=True):
    r_mean, r_rstd, r_bt, r_scr = rt
    inv_n = 1.0 / n
    if sub_mean:
        e0 = b.vts(r_mean[0:1, 0:width], ps_stat[0:1, 0:width], inv_n, None,
                   ALU.mult, deps=(ev_stat,))
        e1 = b.vts(r_scr[0:1, 0:width], ps_stat[0:1, width:2 * width], inv_n,
                   None, ALU.mult, deps=(ev_stat,))
        e2 = b.vtt(r_rstd[0:1, 0:width], r_mean[0:1, 0:width],
                   r_mean[0:1, 0:width], ALU.mult, deps=(e0,))
        e3 = b.vtt(r_scr[0:1, 0:width], r_scr[0:1, 0:width],
                   r_rstd[0:1, 0:width], ALU.subtract, deps=(e1, e2))
    else:
        b.wait(b.v, ev_stat)
        e0 = b.mark(b.v.memset(r_mean[0:1, 0:width], 0.0), "ve")
        e3 = b.vts(r_scr[0:1, 0:width], ps_stat[0:1, width:2 * width], inv_n,
                   None, ALU.mult, deps=(ev_stat,))
    e4 = b.vts(r_rstd[0:1, 0:width], r_scr[0:1, 0:width], eps, None, ALU.add,
               deps=(e3,))
    b.wait(b.v, e4)
    e5 = b.mark(b.v.reciprocal(r_scr[0:1, 0:width], r_rstd[0:1, 0:width]), "ve")
    e6 = b.act(r_rstd[0:1, 0:width], r_scr[0:1, 0:width], AF.Sqrt, deps=(e5,))
    if sub_mean:
        e7 = b.vstt(r_bt[0:1, 0:width], r_mean[0:1, 0:width], -1.0,
                    r_rstd[0:1, 0:width], ALU.mult, ALU.mult, deps=(e6, e0))
    else:
        b.wait(b.v, e6)
        e7 = b.mark(b.v.memset(r_bt[0:1, 0:width], 0.0), "ve")
    return e7


def bcast_f32(b, ps_dst, onesf, src_row, w, off=0, deps=()):
    ev = None
    bp = src_row.base_partition()
    b.wait(b.t, *deps)
    for tb0 in range(0, w, TB):
        e = b.t.matmul(ps_dst[:, tb0:tb0 + TB], onesf[bp:bp + 1, :],
                       src_row[0:1, off + tb0:off + tb0 + TB], start=True, stop=True)
        ev = b.mark(e, "mm")
    return ev



def build_launch_A():
    nc = bass.Bass("TRN2", target_bir_lowering=False)
    b = Bld(nc)

    def din(name, shape, dt=F32):
        return nc.dram_tensor(name, list(shape), dt, kind="ExternalInput").ap()

    def dout(name, shape, dt=F32):
        return nc.dram_tensor(name, list(shape), dt, kind="ExternalOutput").ap()

    x0 = din("x0", [KC * 128, T])
    memT = din("memT", [1024, 2048])
    pos = din("pos", [1, T])
    wq0 = din("wq0", [8, KC, 128, 128])
    wk0 = din("wk0", [8, 8, 128, 128])
    wv0r = din("wv0r", [2, 8, 128, TB])
    wo0 = din("wo0", [KC, 8, 128, 128])
    qw1 = din("qw1", [KC, KC, 128, 128])
    kw1 = din("kw1", [4, KC, 128, 128])
    vw1r = din("vw1r", [1, KC, 128, TB])
    wmod0 = din("wmod0", [24, KC, 128, TB])
    c0col = din("c0col", [128, KC])
    gs_col = din("gs_col", [128, KC])
    bs_col = din("bs_col", [128, KC])
    gm_col = din("gm_col", [128, 8])
    bm_col = din("bm_col", [128, 8])
    log_beta = din("log_beta", [1, 8])
    anw_col = din("anw_col", [128, KC])
    qnw_col = din("qnw_col", [128, 1])
    knw_col = din("knw_col", [128, 1])

    x1_o = dout("x1", [KC * 128, T])
    h1_o = dout("h1", [KC * 128, T])
    q1_o = dout("Q1", [KC * 128, T], BF16)
    k1_o = dout("K1", [4 * 128, T], BF16)
    v1_o = dout("V1", [T, 512], BF16)
    mod0_o = dout("mod0", [1, 6 * D])
    mcols_o = dout("mcolsA", [128, 6 * KC])

    modrow_h = nc.dram_tensor("modrow_h", [1, 6 * D], F32).ap()
    mcols_h = nc.dram_tensor("mcols_h", [6 * D], F32).ap()

    sfull = b.sb("sfull", [128, 4 * 2048], F32)
    bb0 = b.sb("bb0", [128, KC * T], BF16)
    bb1 = b.sb("bb1", [128, KC * T], BF16)
    bb2 = b.sb("bb2", [128, KC * T], BF16)
    xib = b.sb("xib", [128, 8 * T], BF16)
    xsA = b.sb("xsA", [128, 2048], F32)
    xsB = b.sb("xsB", [128, 2048], F32)
    xsq = b.sb("xsq", [128, 1024], F32)
    sqb = b.sb("sqb", [128, 2048], BF16)
    tbf = b.sb("tbf", [128, 2048], BF16)
    wsl0 = b.sb("wsl0", [128, KC * 128], BF16)
    wsl1 = b.sb("wsl1", [128, KC * 128], BF16)
    rowsf = b.sb("rowsf", [128, 5 * 1024], F32)
    r_mean = rowsf[0:1, 0:1024]
    r_rstd = rowsf[0:1, 1024:2048]
    r_bt = rowsf[0:1, 2048:3072]
    r_scr = rowsf[0:1, 3072:4096]
    r_scr2 = rowsf[0:1, 4096:5120]
    erows = b.sb("erows", [128, 64], F32)
    ident = b.sb("ident", [128, 128], BF16)
    onescol = b.sb("onescol", [128, 1], BF16)
    onesf = b.sb("onesf", [128, 128], F32)
    mrow = b.sb("mrow", [1, 512], F32)
    mcols = b.sb("mcols", [128, 6 * KC], F32)
    beta_bc = b.sb("beta_bc", [128, 8], F32)
    ccol = b.sb("ccol", [128, KC], F32)
    ccol_bf = b.sb("ccol_bf", [128, KC], BF16)
    
    vcols = b.sb("vcols", [128, KC], F32)

    wsl_last = [[], []]

    ps_big = b.ps("ps_big", [128, 2048])
    ps_a = b.ps("ps_a", [128, 512])
    ps_b2 = b.ps("ps_b2", [128, 512])
    ps_t = b.ps("ps_t", [128, 512], BF16)
    ps_pv = b.ps("ps_pv", [128, 128])

    ropeC = sfull[:, 4096:4096 + T]
    ropeS = sfull[:, 5120:5120 + T]
    posv = sfull[0:1, 6144:6144 + T]
    omv = sfull[0:1, 7168:7168 + 128]
    ropeT = sfull[:, 2048:2048 + T]

    kint = b.sb("kint", [128, 1024], mybir.dt.int32)
    constt = b.sb("constt", [128, 8], F32)
    _consts = {}
    _cev = [None]

    def constf(val):
        val = float(np.float32(val))
        if val not in _consts:
            j = len(_consts)
            _consts[val] = j
            nc.const_aps.register(F32, val, constt[:, j:j + 1])
            _cev[0] = b.mark(b.g.memset(constt[:, j:j + 1], val), "gc")
        return val

    b.g.memset(erows[:], 0.0)
    ev_id = make_ident(b, ident[:])
    b.g.memset(onescol[:], 1.0)
    b.g.memset(onesf[:], 1.0)
    ev_ones = None

    ev_c = b.dma(b.g, ccol[:], c0col)
    ev_cb = b.act(ccol_bf[:], ccol[:], AF.Silu, deps=(ev_c,))
    ev_w_par = [None, None]
    ev_mm_par = [None, None]
    ev_row_prev = None
    for nb in range(24):
        for kg in range(4):
            par = kg % 2
            wsl = wsl0 if par == 0 else wsl1
            b.wait(b.g, ev_mm_par[par], *wsl_last[par])
            ev_w = b.dma(b.g, wsl[:, 0:4 * TB].rearrange("p (k f) -> p k f", k=4),
                         wmod0[nb, kg * 4:(kg + 1) * 4].rearrange("k p f -> p k f"), sem=f"gw{par}")
            ev_w_par[par] = ev_w
            b.wait(b.t, ev_w, ev_cb, ev_row_prev)
            for kk in range(4):
                kc = kg * 4 + kk
                i = b.t.matmul(ps_a[0:1, 0:TB],
                               ccol_bf[:, kc:kc + 1], wsl[:, kk * TB:(kk + 1) * TB],
                               start=(kc == 0), stop=(kc == KC - 1))
            ev_mm = b.mark(i, "mm")
            ev_mm_par[par] = ev_mm
            wsl_last[par] = [ev_mm]
        ev_row = b.vcp(mrow[0:1, 0:TB], ps_a[0:1, 0:TB], deps=(ev_mm,))
        b.wait(b.y, ev_row)
        ev_st = b.dma(b.y, modrow_h[0:1, nb * TB:(nb + 1) * TB], mrow[0:1, 0:TB])
        ev_st2 = b.dma(b.y, mod0_o[0:1, nb * TB:(nb + 1) * TB], mrow[0:1, 0:TB])
        ev_row_prev = Ev("ydma", max(ev_st.val, ev_st2.val))
    ev_g = None
    for vi in range(6):
        for half in range(2):
            hof = half * 1024
            ev_ld = b.dma(b.g, r_rstd[0:1, 0:1024],
                          modrow_h[0:1, vi * D + hof: vi * D + hof + 1024])
            srcr = r_rstd[0:1, 0:1024]
            if vi in (2, 5):
                e0 = b.act(r_bt[0:1, 0:1024], srcr, AF.Abs, deps=(ev_ld,))
                e1 = b.vts(r_bt[0:1, 0:1024], r_bt[0:1, 0:1024], 0.25, None,
                           ALU.max, deps=(e0,))
                e2 = b.act(r_scr[0:1, 0:1024], srcr, AF.Sign, deps=(ev_ld,))
                e3 = b.vtt(r_rstd[0:1, 0:1024], r_bt[0:1, 0:1024],
                           r_scr[0:1, 0:1024], ALU.mult, deps=(e1, e2))
                b.wait(b.y, e3)
            b.wait(b.y, Ev("gdma", ev_ld.val))
            ev_g = b.dma(b.y, mcols_h[vi * D + hof: vi * D + hof + 1024]
                         .rearrange("(o a) -> o a", o=1), r_rstd[0:1, 0:1024])
            b.wait(b.g, ev_g)
    for vi in range(6):
        for kc in range(KC):
            ev_mc = b.dma(b.g, mcols[:, vi * KC + kc: vi * KC + kc + 1],
                          mcols_h[vi * D + kc * 128: vi * D + (kc + 1) * 128]
                          .rearrange("(p o) -> p o", o=1))

    b.wait(b.y, Ev("gdma", b.cnt["gdma"]))
    b.dma(b.y, mcols_o, mcols[:])

    def mcol(vi, kc):
        return mcols[:, vi * KC + kc: vi * KC + kc + 1]

    xev = [[], []]

    def x_chunk(kc, par, deps=()):
        buf = xsA if par == 0 else xsB
        e = b.dma(b.g, buf[:, 0:T], x0[kc * 128:(kc + 1) * 128, :], sem=f"gx{par}")
        return buf, e

    ev_stat = None
    ev_sq_rd = None
    ev_mean_rd = None
    for kc in range(KC):
        par = kc % 2
        b.wait(b.g, *xev[par])
        buf, e_x = x_chunk(kc, par)
        ev_sq = b.act(sqb[:, 0:T], buf[:, 0:T], AF.Square, deps=(e_x, ev_sq_rd))
        ev_cp = b.vcp(tbf[:, 0:T], buf[:, 0:T], deps=(e_x, ev_mean_rd))
        b.wait(b.t, ev_sq, ev_cp)
        for tb0 in range(0, T, TB):
            b.mm(ps_big[0:1, tb0:tb0 + TB], onescol[:, 0:1], tbf[:, tb0:tb0 + TB],
                 start=(kc == 0), stop=(kc == KC - 1))
            i = b.t.matmul(ps_big[0:1, T + tb0:T + tb0 + TB], onescol[:, 0:1],
                           sqb[:, tb0:tb0 + TB], start=(kc == 0),
                           stop=(kc == KC - 1))
        ev_last = b.mark(i, "mm")
        ev_sq_rd = ev_last
        ev_mean_rd = ev_last
        xev[par] = [ev_sq, ev_cp]
    ev_stat = ev_last
    ev_rows = row_stats(b, ps_big, T, D, 1e-5, (r_mean, r_rstd, r_bt, r_scr), ev_stat, sub_mean=True)
    ev_bcR = bcast_f32(b, ps_big[:, 0:T], onesf, r_rstd, T, deps=(ev_rows, ev_stat))
    ev_bcB = bcast_f32(b, ps_big[:, T:2 * T], onesf, r_bt, T, deps=(ev_rows,))
    fence = (Ev("se", b.cnt["se"]), Ev("ve", b.cnt["ve"]))
    xev = [list(fence), list(fence)]
    for kc in range(KC):
        par = kc % 2
        b.wait(b.g, *xev[par])
        buf, e_x = x_chunk(kc, par)
        obuf = xsB if par == 0 else xsA
        e1 = b.vtt(xsq[:, 0:T], buf[:, 0:T], ps_big[:, 0:T], ALU.mult,
                   deps=(e_x, ev_bcR, ev_mc, *xev[1 - par]))
        e2 = b.vtt(obuf[:, 0:T], xsq[:, 0:T], ps_big[:, T:2 * T], ALU.add,
                   deps=(e1, ev_bcB))
        e3 = b.vts(vcols[:, kc:kc + 1], mcol(1, kc), 1.0, None, ALU.add,
                   deps=(ev_mc,))
        e4 = b.vts(bb0[:, kc * T:(kc + 1) * T], obuf[:, 0:T],
                   vcols[:, kc:kc + 1], mcol(0, kc), ALU.mult, ALU.add,
                   deps=(e2, e3))
        xev[par] = [e1]
        xev[1 - par] = xev[1 - par] + [e4]
    ev_h = xev[0] if xev[0] else None
    ev_h_all = [xev[0], xev[1]]


    b.wait(b.g, Ev("ve", b.cnt["ve"]))
    ev_gs = b.dma(b.g, ccol[:], gs_col)
    ev_bs = b.dma(b.g, vcols[:], bs_col)
    ev_sq_rd = None
    for kc in range(KC):
        sl = bb0[:, kc * T:(kc + 1) * T]
        ev_sq = b.act(sqb[:, 0:T], sl, AF.Square, deps=(ev_h_all[0], ev_h_all[1], ev_sq_rd))
        b.wait(b.t, ev_sq, ev_h_all[0], ev_h_all[1], ev_bcR, ev_bcB)
        for tb0 in range(0, T, TB):
            b.mm(ps_big[0:1, tb0:tb0 + TB], onescol[:, 0:1], sl[:, tb0:tb0 + TB],
                 start=(kc == 0), stop=(kc == KC - 1))
            i = b.t.matmul(ps_big[0:1, T + tb0:T + tb0 + TB], onescol[:, 0:1],
                           sqb[:, tb0:tb0 + TB], start=(kc == 0),
                           stop=(kc == KC - 1))
        ev_sq_rd = b.mark(i, "mm")
    ev_stat = ev_sq_rd
    ev_rows = row_stats(b, ps_big, T, D, 1e-5, (r_mean, r_rstd, r_bt, r_scr), ev_stat, sub_mean=True)
    ev_bcR = bcast_f32(b, ps_big[:, 0:T], onesf, r_rstd, T, deps=(ev_rows, ev_stat))
    ev_bcB = bcast_f32(b, ps_big[:, T:2 * T], onesf, r_bt, T, deps=(ev_rows,))
    hn_ev = []
    prev = None
    for kc in range(KC):
        sl = bb0[:, kc * T:(kc + 1) * T]
        e1 = b.vtt(xsq[:, 0:T], sl, ps_big[:, 0:T], ALU.mult,
                   deps=(ev_bcR, prev))
        e2 = b.vtt(xsA[:, 0:T], xsq[:, 0:T], ps_big[:, T:2 * T], ALU.add,
                   deps=(e1, ev_bcB))
        e3 = b.vts(bb1[:, kc * T:(kc + 1) * T], xsA[:, 0:T],
                   ccol[:, kc:kc + 1], vcols[:, kc:kc + 1], ALU.mult, ALU.add,
                   deps=(e2, ev_gs, ev_bs))
        hn_ev.append(e3)
        prev = e3
    ev_hn = prev


    b.wait(b.g, ev_hn, Ev("ve", b.cnt["ve"]))
    ev_gm = b.dma(b.g, ccol[:, 0:8], gm_col)
    ev_bm = b.dma(b.g, vcols[:, 0:8], bm_col)
    mem_ev = []
    ev_sq_rd = None
    ev_ld_prev = [list(xev[0]) + list(xev[1]), list(xev[0]) + list(xev[1])]
    for kc in range(8):
        par = kc % 2
        buf = xsA if par == 0 else xsB
        b.wait(b.g, *ev_ld_prev[par])
        e_m = b.dma(b.g, buf[:, 0:2048], memT[kc * 128:(kc + 1) * 128, :],
                    sem=f"gx{par}")
        e_cp = b.vcp(bb2[:, kc * 2048:(kc + 1) * 2048], buf[:, 0:2048], deps=(e_m,))
        ev_sq = b.act(sqb[:, 0:2048], buf[:, 0:2048], AF.Square,
                      deps=(e_m, ev_sq_rd))
        b.wait(b.t, ev_sq, e_cp, ev_hn)
        for tb0 in range(0, 2048, TB):
            b.mm(ps_big[0:1, tb0:tb0 + TB], onescol[:, 0:1],
                 bb2[:, kc * 2048 + tb0:kc * 2048 + tb0 + TB],
                 start=(kc == 0), stop=(kc == 7))
            i = b.t.matmul(ps_big[32:33, tb0:tb0 + TB], onescol[:, 0:1],
                           sqb[:, tb0:tb0 + TB], start=(kc == 0),
                           stop=(kc == 7))
        ev_sq_rd = b.mark(i, "mm")
        ev_ld_prev[par] = [ev_sq, e_cp]
        mem_ev.append(e_cp)
    ev_stat = ev_sq_rd
    inv_n = 1.0 / 1024.0
    mn_ev = []
    for half in range(2):
        hsl = slice(half * 1024, (half + 1) * 1024)
        e0 = b.vts(r_mean[0:1, 0:1024], ps_big[0:1, hsl], inv_n, None, ALU.mult,
                   deps=(ev_stat,))
        e1 = b.vts(r_scr[0:1, 0:1024], ps_big[32:33, hsl], inv_n, None, ALU.mult,
                   deps=(ev_stat,))
        e2 = b.vtt(r_rstd[0:1, 0:1024], r_mean[0:1, 0:1024], r_mean[0:1, 0:1024],
                   ALU.mult, deps=(e0,))
        e3 = b.vtt(r_scr[0:1, 0:1024], r_scr[0:1, 0:1024], r_rstd[0:1, 0:1024],
                   ALU.subtract, deps=(e1, e2))
        e4 = b.vts(r_rstd[0:1, 0:1024], r_scr[0:1, 0:1024], 1e-5, None, ALU.add,
                   deps=(e3,))
        b.wait(b.v, e4)
        e5 = b.mark(b.v.reciprocal(r_scr[0:1, 0:1024], r_rstd[0:1, 0:1024]), "ve")
        e6 = b.act(r_rstd[0:1, 0:1024], r_scr[0:1, 0:1024], AF.Sqrt, deps=(e5,))
        e7 = b.vstt(r_bt[0:1, 0:1024], r_mean[0:1, 0:1024], -1.0,
                    r_rstd[0:1, 0:1024], ALU.mult, ALU.mult, deps=(e6,))
        evR = bcast_f32(b, ps_big[:, hsl], onesf, r_rstd, 1024,
                        deps=(e7, ev_stat))
        prev = None
        tmp_ev = []
        for kc in range(8):
            sl_in = bb2[:, kc * 2048 + half * 1024: kc * 2048 + (half + 1) * 1024]
            e1_ = b.vtt(xsq[:, 0:1024], sl_in, ps_big[:, hsl], ALU.mult,
                        deps=(evR, prev, mem_ev[kc]))
            e2_ = b.vcp(sfull[:, kc * 1024:(kc + 1) * 1024], xsq[:, 0:1024],
                        deps=(e1_,))
            prev = e2_
            tmp_ev.append(e2_)
        evB = bcast_f32(b, ps_big[:, hsl], onesf, r_bt, 1024,
                        deps=(e7, tmp_ev[-1], prev))
        prev4 = None
        for kc in range(8):
            e3_ = b.vtt(xsq[:, 0:1024], sfull[:, kc * 1024:(kc + 1) * 1024],
                        ps_big[:, hsl], ALU.add,
                        deps=(evB, tmp_ev[kc], prev4))
            e4_ = b.vts(bb0[:, kc * 2048 + half * 1024: kc * 2048 + (half + 1) * 1024],
                        xsq[:, 0:1024], ccol[:, kc:kc + 1], vcols[:, kc:kc + 1],
                        ALU.mult, ALU.add, deps=(e3_, ev_gm, ev_bm))
            prev4 = e4_
            mn_ev.append(e4_)
    ev_mn = mn_ev[-1]

    ev_lb = b.dma(b.g, mrow[0:1, 0:8], log_beta)
    ev_eb = b.act(r_scr2[0:1, 0:8], mrow[0:1, 0:8], AF.Exp, deps=(ev_lb,))
    b.wait(b.t, ev_eb)
    ev_bb = b.mark(b.t.matmul(ps_pv[:, 0:8], onesf[0:1, :], r_scr2[0:1, 0:8],
                              start=True, stop=True), "mm")
    ev_beta = b.vcp(beta_bc[:], ps_pv[:, 0:8], deps=(ev_bb,))

    def gemm(w_hbm, noc, nkc, x_sb, x_cols, epilogue, deps=()):
        ev_ep = None
        for oc in range(noc):
            par = oc % 2
            wsl = wsl0 if par == 0 else wsl1
            b.wait(b.g, *wsl_last[par])
            ev_w = b.dma(b.g, wsl[:, 0:nkc * 128].rearrange("p (k f) -> p k f", k=nkc),
                         w_hbm[oc].rearrange("k p f -> p k f"), sem=f"gw{par}")
            for tbi in range(x_cols // TB):
                pp = tbi % 2
                psd = ps_a if pp == 0 else ps_b2
                b.wait(b.t, ev_w, *deps)
                for kc in range(nkc):
                    i = b.t.matmul(psd[:, 0:TB],
                                   wsl[:, kc * 128:(kc + 1) * 128],
                                   x_sb[:, kc * x_cols + tbi * TB: kc * x_cols + tbi * TB + TB],
                                   start=(kc == 0), stop=(kc == nkc - 1))
                ev_mm = b.mark(i, "mm")
                wsl_last[par] = [ev_mm]
                ev_ep = epilogue(oc, tbi, psd, ev_mm)
                b.wait(b.t, ev_ep)
        return ev_ep

    def ep_q(oc, tbi, psd, ev_mm):
        return b.vcp(xib[:, oc * T + tbi * TB: oc * T + tbi * TB + TB],
                     psd[:, 0:TB], deps=(ev_mm,))
    gemm(wq0, 8, KC, bb1, T, ep_q, deps=(ev_hn,))

    def ep_k(oc, tbi, psd, ev_mm):
        return b.vts(bb1[:, oc * 2048 + tbi * TB: oc * 2048 + tbi * TB + TB],
                     psd[:, 0:TB], beta_bc[:, oc:oc + 1], None, ALU.mult,
                     deps=(ev_mm, ev_beta))
    gemm(wk0, 8, 8, bb0, 2048, ep_k, deps=(ev_mn,))

    ev_use = None
    ev_wv_par = [None, None]
    for ob in range(2):
        for st in range(16):
            psd = ps_a if st % 2 == 0 else ps_b2
            for kg in range(2):
                par = kg % 2
                wsl = wsl0 if par == 0 else wsl1
                b.wait(b.g, *wsl_last[par])
                ev_w = b.dma(b.g, wsl[:, 0:4 * TB].rearrange("p (k f) -> p k f", k=4),
                             wv0r[ob, kg * 4:(kg + 1) * 4].rearrange("k p f -> p k f"), sem=f"gw{par}")
                b.wait(b.t, ev_w, ev_mn)
                for kk in range(4):
                    kc = kg * 4 + kk
                    i = b.t.matmul(psd[:, 0:TB],
                                   bb0[:, kc * 2048 + st * 128: kc * 2048 + st * 128 + 128],
                                   wsl[:, kk * TB:(kk + 1) * TB],
                                   start=(kc == 0), stop=(kc == 7))
                ev_wv_par[par] = b.mark(i, "mm")
                wsl_last[par] = [ev_wv_par[par]]
            ev_mm = ev_wv_par[1]
            dst = bass.AP(bb2.ap().tensor, ob * 4 * 2048 + st * 128,
                          [[KC * T, 128], [2048, 4], [1, 128]])
            ev_use = b.vcp(dst, psd[:, 0:TB].rearrange("p (h f) -> p h f", h=4),
                           deps=(ev_mm,))
            b.wait(b.t, ev_use)
    ev_v = ev_use

    EC_TAU, EC_B, EC_A, EC_C, EC_MX, EC_S1, EC_S2, EC_S3 = 0, 8, 16, 24, 32, 40, 48, 56

    def erow(base, t_, w=1):
        return erows[:, base + t_: base + t_ + w]

    last_pv_ev = Ev("gdma", b.cnt["gdma"])
    last_sc_read = None
    for h in range(8):
        for step in range(3):
            for tg in range(2):
                t0g = tg * 4
                sc_ev = []
                for tt_ in range(4):
                    t_ = t0g + tt_
                    deps = [ev_v, ev_beta]
                    if last_sc_read is not None:
                        deps.append(last_sc_read)
                    b.wait(b.t, *deps)
                    for nb in range(4):
                        i = b.t.matmul(ps_big[:, nb * TB:(nb + 1) * TB],
                                       xib[:, h * T + t_ * 128: h * T + t_ * 128 + 128],
                                       bb1[:, h * 2048 + nb * TB: h * 2048 + (nb + 1) * TB],
                                       start=True, stop=True)
                    ev_mm = b.mark(i, "mm")
                    e_ev = b.vts(sfull[:, tt_ * 2048:(tt_ + 1) * 2048], ps_big[:, 0:2048],
                                 0.5, None, ALU.mult, deps=(ev_mm,))
                    e_mx = b.red(erow(EC_MX, tt_), ps_big[:, 0:2048], ALU.max,
                                 deps=(ev_mm,))
                    last_sc_read = Ev("ve", max(e_ev.val, e_mx.val))
                    sc_ev.append(last_sc_read)
                ev_tau = b.vts(erow(EC_TAU, 0, 4), erow(EC_MX, 0, 4), 0.5, -1.0,
                               ALU.mult, ALU.add, deps=tuple(sc_ev))

                for it in range(N_AS_ITERS):
                    ev_sq_list = []
                    ev_relu_prev = [None, None]
                    ev_sq_prev = None
                    for tt_ in range(4):
                        par = tt_ % 2
                        tscr = xsA if par == 0 else xsB
                        e_r = b.vts(tscr[:, 0:2048], sfull[:, tt_ * 2048:(tt_ + 1) * 2048],
                                    erow(EC_TAU, tt_), 0.0, ALU.subtract, ALU.max,
                                    deps=(ev_tau, ev_relu_prev[par],
                                          ev_relu_prev[1 - par], ev_sq_prev))
                        e_b = b.vts(tbf[:, 0:2048], tscr[:, 0:2048], 0.0, 0.0,
                                    ALU.add, ALU.add, accum=erow(EC_B, tt_),
                                    deps=(e_r,))
                        oscr = xsB if par == 0 else xsA
                        e_a = b.vstt(oscr[:, 0:2048], tscr[:, 0:2048], 1.0,
                                     tscr[:, 0:2048], ALU.mult, ALU.mult,
                                     accum=erow(EC_A, tt_),
                                     deps=(e_r, ev_relu_prev[par]))
                        e_q = b.act(sqb[:, 0:2048], tscr[:, 0:2048], AF.Sign,
                                    accum=erow(EC_C, tt_), deps=(e_r, ev_sq_prev))
                        ev_relu_prev[par] = Ev("ve", e_a.val)
                        ev_sq_prev = e_q
                        ev_sq_list.append(Ev("se", e_q.val))
                        ev_sq_list.append(e_b)
                        ev_sq_list.append(e_a)
                    dd = tuple(ev_sq_list)
                    e1 = b.vts(erow(EC_S1, 0, 4), erow(EC_C, 0, 4), 1.0, None,
                               ALU.max, deps=dd)
                    b.wait(b.v, e1)
                    e2 = b.mark(b.v.reciprocal(erow(EC_S2, 0, 4), erow(EC_S1, 0, 4)), "ve")
                    e3 = b.vtt(erow(EC_S1, 0, 4), erow(EC_B, 0, 4), erow(EC_S2, 0, 4),
                               ALU.mult, deps=(e2,))
                    e4 = b.vtt(erow(EC_S3, 0, 4), erow(EC_B, 0, 4), erow(EC_S1, 0, 4),
                               ALU.mult, deps=(e3,))
                    e5 = b.vtt(erow(EC_S3, 0, 4), erow(EC_A, 0, 4), erow(EC_S3, 0, 4),
                               ALU.subtract, deps=(e4,))
                    e6 = b.vts(erow(EC_S3, 0, 4), erow(EC_S3, 0, 4), -1.0, 1.0,
                               ALU.mult, ALU.add, deps=(e5,))
                    e7 = b.vtt(erow(EC_S3, 0, 4), erow(EC_S3, 0, 4), erow(EC_S2, 0, 4),
                               ALU.mult, deps=(e6,))
                    e8 = b.vts(erow(EC_S3, 0, 4), erow(EC_S3, 0, 4), 0.0, None,
                               ALU.max, deps=(e7,))
                    e9 = b.act(erow(EC_S3, 0, 4), erow(EC_S3, 0, 4), AF.Sqrt,
                               deps=(e8,))
                    e10 = b.vtt(erow(EC_S1, 0, 4), erow(EC_S1, 0, 4), erow(EC_S3, 0, 4),
                                ALU.subtract, deps=(e9,))
                    ev_tau = b.vtt(erow(EC_TAU, 0, 4), erow(EC_TAU, 0, 4),
                                   erow(EC_S1, 0, 4), ALU.add, deps=(e10,))


                for tt_ in range(4):
                    t_ = t0g + tt_
                    e_r = b.vts(tbf[:, 0:2048], sfull[:, tt_ * 2048:(tt_ + 1) * 2048],
                                erow(EC_TAU, tt_), 0.0, ALU.subtract, ALU.max,
                                deps=(ev_tau, last_pv_ev))
                    e_p = b.act(sqb[:, 0:2048], tbf[:, 0:2048], AF.Square,
                                deps=(e_r,))

                    ev_tp_copy = e_r
                    for nb4 in range(4):
                        tp_e = None
                        for j in range(4):
                            n = nb4 * 4 + j
                            b.wait(b.t, e_p, ev_tp_copy, ev_id)
                            tp_e = b.tp(ps_t[:, j * 128:(j + 1) * 128],
                                        sqb[:, n * 128:(n + 1) * 128], ident[:],
                                        markit=(j == 3))
                        ev_cp = b.vcp(tbf[:, 0:512], ps_t[:, 0:512], deps=(tp_e,))
                        ev_tp_copy = ev_cp
                        b.wait(b.t, ev_cp)
                        for j in range(4):
                            n = nb4 * 4 + j
                            i = b.t.matmul(ps_pv[:, 0:128],
                                           bb2[:, h * 2048 + n * 128: h * 2048 + (n + 1) * 128],
                                           tbf[:, j * 128:(j + 1) * 128],
                                           start=(n == 0), stop=(n == 15))
                    ev_pv = b.mark(i, "mm")
                    last_pv_ev = b.vcp(xib[:, h * T + t_ * 128: h * T + t_ * 128 + 128],
                                       ps_pv[:, 0:128], deps=(ev_pv,))
                    last_sc_read = Ev("ve", max(last_sc_read.val, last_pv_ev.val))


    ev_x1 = []
    ev_stat_last = None

    _fence_wo = [Ev("ve", b.cnt["ve"]), Ev("se", b.cnt["se"])]
    wo_prev = [list(_fence_wo), list(_fence_wo)]
    wo_store = [None]

    def ep_wo(oc, tbi, psd, ev_mm):
        nonlocal ev_stat_last
        par = (oc * 2 + tbi) % 2
        buf = xsA if par == 0 else xsB
        b.wait(b.g, *wo_prev[par])
        e_x = b.dma(b.g, buf[:, 0:TB], x0[oc * 128:(oc + 1) * 128, tbi * TB:(tbi + 1) * TB],
                    sem=f"gx{par}")
        e1 = b.vstt(xsq[:, 0:TB], psd[:, 0:TB], mcol(2, oc), buf[:, 0:TB],
                    ALU.mult, ALU.add,
                    deps=(ev_mm, e_x, ev_mc, wo_store[0]))
        wo_prev[par] = [e1]
        b.wait(b.y, e1)
        e_st = b.dma(b.y, x1_o[oc * 128:(oc + 1) * 128, tbi * TB:(tbi + 1) * TB],
                     xsq[:, 0:TB])
        wo_store[0] = Ev("ydma", e_st.val)
        e_sq = b.act(sqb[:, 0:TB], xsq[:, 0:TB], AF.Square, deps=(e1,))
        e_cp = b.vcp(bb0[:, oc * T + tbi * TB: oc * T + tbi * TB + TB],
                     xsq[:, 0:TB], deps=(e1,))
        b.wait(b.t, e_sq, e_cp)
        first = (oc == 0)
        last = (oc == KC - 1)
        b.mm(ps_big[0:1, tbi * TB:(tbi + 1) * TB], onescol[:, 0:1],
             bb0[:, oc * T + tbi * TB: oc * T + tbi * TB + TB],
             start=first, stop=last)
        i = b.t.matmul(ps_big[32:33, tbi * TB:(tbi + 1) * TB], onescol[:, 0:1],
                       sqb[:, 0:TB], start=first, stop=last)
        ev = b.mark(i, "mm")
        if last:
            ev_stat_last = ev if ev_stat_last is None else Ev("mm", max(ev.val, ev_stat_last.val))
        return Ev("mm", ev.val)

    gemm(wo0, KC, 8, xib, T, ep_wo, deps=(last_pv_ev,))
    inv_n = 1.0 / D
    st_sum = ps_big[0:1, 0:T]
    st_sq = ps_big[32:33, 0:T]
    e0 = b.vts(r_mean[0:1, 0:T], st_sum, inv_n, None, ALU.mult, deps=(ev_stat_last,))
    e1 = b.vts(r_scr[0:1, 0:T], st_sq, inv_n, None, ALU.mult, deps=(ev_stat_last,))
    e2 = b.vtt(r_rstd[0:1, 0:T], r_mean[0:1, 0:T], r_mean[0:1, 0:T], ALU.mult, deps=(e0,))
    e3 = b.vtt(r_scr[0:1, 0:T], r_scr[0:1, 0:T], r_rstd[0:1, 0:T], ALU.subtract,
               deps=(e1, e2))
    e4 = b.vts(r_rstd[0:1, 0:T], r_scr[0:1, 0:T], 1e-5, None, ALU.add, deps=(e3,))
    b.wait(b.v, e4)
    e5 = b.mark(b.v.reciprocal(r_scr[0:1, 0:T], r_rstd[0:1, 0:T]), "ve")
    e6 = b.act(r_rstd[0:1, 0:T], r_scr[0:1, 0:T], AF.Sqrt, deps=(e5,))
    e7 = b.vstt(r_bt[0:1, 0:T], r_mean[0:1, 0:T], -1.0, r_rstd[0:1, 0:T],
                ALU.mult, ALU.mult, deps=(e6,))
    ev_bcR = bcast_f32(b, ps_big[:, 0:T], onesf, r_rstd, T, deps=(e7, ev_stat_last, last_sc_read))
    ev_bcB = bcast_f32(b, ps_big[:, T:2 * T], onesf, r_bt, T, deps=(e7,))
    b.wait(b.v, wo_store[0])
    prev = None
    h1_ev = []
    for kc in range(KC):
        sl = bb0[:, kc * T:(kc + 1) * T]
        e1_ = b.vtt(xsq[:, 0:T], sl, ps_big[:, 0:T], ALU.mult,
                    deps=(ev_bcR, prev))
        e2_ = b.vtt(xsB[:, 0:T], xsq[:, 0:T], ps_big[:, T:2 * T], ALU.add,
                    deps=(e1_, ev_bcB))
        e3_ = b.vts(vcols[:, kc:kc + 1], mcol(4, kc), 1.0, None, ALU.add,
                    deps=(ev_mc,))
        e4_ = b.vts(xsA[:, 0:T], xsB[:, 0:T], vcols[:, kc:kc + 1],
                    mcol(3, kc), ALU.mult, ALU.add, deps=(e2_, e3_))
        b.wait(b.y, e4_)
        e_st = b.dma(b.y, h1_o[kc * 128:(kc + 1) * 128, :], xsA[:, 0:T])
        e5_ = b.vcp(bb1[:, kc * T:(kc + 1) * T], xsA[:, 0:T], deps=(e4_,))
        b.wait(b.v, Ev("ydma", e_st.val))
        prev = e5_
        h1_ev.append(e5_)
    ev_h1 = prev

    b.wait(b.g, Ev("ve", b.cnt["ve"]))
    ev_anw = b.dma(b.g, ccol[:], anw_col)
    ev_sq_rd = None
    for kc in range(KC):
        sl = bb1[:, kc * T:(kc + 1) * T]
        ev_sq = b.act(sqb[:, 0:T], sl, AF.Square, deps=(ev_h1, ev_sq_rd))
        b.wait(b.t, ev_sq, ev_h1, ev_bcR, ev_bcB)
        for tb0 in range(0, T, TB):
            i = b.t.matmul(ps_big[32:33, tb0:tb0 + TB], onescol[:, 0:1],
                           sqb[:, tb0:tb0 + TB], start=(kc == 0),
                           stop=(kc == KC - 1))
        ev_sq_rd = b.mark(i, "mm")
    ev_stat = ev_sq_rd
    e3 = b.vts(r_scr[0:1, 0:T], ps_big[32:33, 0:T], 1.0 / D, None, ALU.mult,
               deps=(ev_stat,))
    e4 = b.vts(r_rstd[0:1, 0:T], r_scr[0:1, 0:T], 1e-6, None, ALU.add, deps=(e3,))
    b.wait(b.v, e4)
    e5 = b.mark(b.v.reciprocal(r_scr[0:1, 0:T], r_rstd[0:1, 0:T]), "ve")
    e6 = b.act(r_rstd[0:1, 0:T], r_scr[0:1, 0:T], AF.Sqrt, deps=(e5,))
    ev_bcR = bcast_f32(b, ps_big[:, 0:T], onesf, r_rstd, T, deps=(e6, ev_stat))
    prev = None
    for kc in range(KC):
        e1_ = b.vstt(bb2[:, kc * T:(kc + 1) * T], bb1[:, kc * T:(kc + 1) * T],
                     ccol[:, kc:kc + 1], ps_big[:, 0:T], ALU.mult, ALU.mult,
                     deps=(ev_bcR, ev_anw, prev))
        prev = e1_
    ev_hn2 = prev

    b.wait(b.g, Ev("ve", b.cnt["ve"]), Ev("se", b.cnt["se"]))
    ev_pos = b.dma(b.g, posv, pos)
    i_io = b.g.iota(omv[0:1, 0:128], pattern=[[1, 128]], base=0,
                    channel_multiplier=0, allow_small_or_imprecise_dtypes=True)
    ev_io = b.mark(i_io, "gc")
    ev_om = b.act(omv[0:1, 0:128], omv[0:1, 0:128], AF.Exp,
                  scale=float(-np.log(1e6) / 64.0), deps=(ev_io,))

    def build_tables(ev_om, half, bias0):
        b.wait(b.t, ev_om, ev_pos, Ev("ve", b.cnt["ve"]))
        for tb0 in range(0, T, TB):
            e1 = b.t.matmul(ps_big[0:128, tb0:tb0 + TB], omv[0:1, 0:128],
                            posv[0:1, tb0:tb0 + TB], start=True, stop=True)
        ev_ang = b.mark(e1, "mm")
        evs = []
        prevw = None
        for which, bias in ((0, 0.0), (1, HALF_PI)):
            dst = ropeS if which == 0 else ropeC
            e2 = b.vts(xsB[0:half, 0:T], ps_big[0:half, 0:T], bias, None,
                       ALU.add, deps=(ev_ang, prevw))
            e3 = b.vts(xsA[0:half, 0:T], ps_big[0:half, 0:T], 1.0 / TWO_PI_HI,
                       bias / TWO_PI_HI, ALU.mult, ALU.add,
                       deps=(ev_ang, prevw))
            e4 = b.vcp(kint[0:half, 0:T], xsA[0:half, 0:T], deps=(e3,))
            e5 = b.vcp(ropeT[0:half, 0:T], kint[0:half, 0:T], deps=(e4,))
            e5b = b.vtt(xsA[0:half, 0:T], xsA[0:half, 0:T], ropeT[0:half, 0:T],
                        ALU.subtract, deps=(e5,))
            e5c = b.vts(xsA[0:half, 0:T], xsA[0:half, 0:T], 0.5, None, ALU.is_ge,
                        deps=(e5b,))
            e5d = b.vtt(xsA[0:half, 0:T], ropeT[0:half, 0:T], xsA[0:half, 0:T],
                        ALU.add, deps=(e5c,))
            e6 = b.vstt(xsB[0:half, 0:T], xsA[0:half, 0:T], -TWO_PI_HI,
                        xsB[0:half, 0:T], ALU.mult, ALU.add, deps=(e5d, e2))
            e7 = b.vstt(xsB[0:half, 0:T], xsA[0:half, 0:T], -TWO_PI_LO,
                        xsB[0:half, 0:T], ALU.mult, ALU.add, deps=(e6,))
            e8 = b.vts(xsB[0:half, 0:T], xsB[0:half, 0:T], -PI_SAFE, PI_SAFE,
                       ALU.max, ALU.min, deps=(e7,))
            e9 = b.act(dst[0:half, 0:T], xsB[0:half, 0:T], AF.Sin, deps=(e8,))
            prevw = Ev("se", e9.val)
            evs.append(e9)
        return evs

    evS, evC = build_tables(ev_om, 64, 0.0)
    b.wait(b.y, evC, evS)
    e_c2 = b.dma(b.y, ropeC[64:128, 0:T], ropeC[0:64, 0:T])
    e_s2b = b.dma(b.y, ropeS[64:128, 0:T], ropeS[0:64, 0:T])
    e_neg = b.vts(ropeS[0:64, 0:T], ropeS[0:64, 0:T], -1.0, None, ALU.mult,
                  deps=(Ev("ydma", e_s2b.val),))
    ev_tab = Ev("ve", e_neg.val)
    ev_tabC = Ev("ydma", e_c2.val)

    b.wait(b.g, Ev("ve", b.cnt["ve"]))
    ev_qn = b.dma(b.g, vcols[:, 0:1], qnw_col)
    ev_kn = b.dma(b.g, vcols[:, 1:2], knw_col)

    def head_postproc(oc, psd_pair, ev_pair, nw_col, out_hbm, ev_nw):
        eA = b.vcp(xsA[:, 0:TB], psd_pair[0][:, 0:TB], deps=(ev_pair[0],))
        eB = b.vcp(xsA[:, TB:2 * TB], psd_pair[1][:, 0:TB], deps=(ev_pair[1],))
        e_sq = b.act(sqb[:, 0:T], xsA[:, 0:T], AF.Square, deps=(eA, eB))
        b.wait(b.t, e_sq)
        for tb0 in range(0, T, TB):
            i = b.t.matmul(ps_big[32:33, tb0:tb0 + TB], onescol[:, 0:1],
                           sqb[:, tb0:tb0 + TB], start=True, stop=True)
        ev_st = b.mark(i, "mm")
        e3 = b.vts(r_scr[0:1, 0:T], ps_big[32:33, 0:T], 1.0 / 128, None, ALU.mult,
                   deps=(ev_st,))
        e4 = b.vts(r_rstd[0:1, 0:T], r_scr[0:1, 0:T], 1e-6, None, ALU.add,
                   deps=(e3,))
        b.wait(b.v, e4)
        e5 = b.mark(b.v.reciprocal(r_scr[0:1, 0:T], r_rstd[0:1, 0:T]), "ve")
        e6 = b.act(r_rstd[0:1, 0:T], r_scr[0:1, 0:T], AF.Sqrt, deps=(e5,))
        evR = bcast_f32(b, ps_big[:, 0:T], onesf, r_rstd, T, deps=(e6, ev_st))
        e7 = b.vstt(xsB[:, 0:T], xsA[:, 0:T], nw_col, ps_big[:, 0:T],
                    ALU.mult, ALU.mult, deps=(evR, ev_nw))
        b.wait(b.y, e7)
        eswp1 = b.dma(b.y, xsA[0:64, 0:T], xsB[64:128, 0:T])
        eswp2 = b.dma(b.y, xsA[64:128, 0:T], xsB[0:64, 0:T])
        esw = Ev("ydma", max(eswp1.val, eswp2.val))
        p1 = b.vtt(xsq[:, 0:T], xsB[:, 0:T], ropeC[:, 0:T], ALU.mult,
                   deps=(e7, ev_tabC, ev_tab))
        p2 = b.vtt(xsB[:, 0:T], xsA[:, 0:T], ropeS[:, 0:T], ALU.mult,
                   deps=(esw, ev_tab, p1))
        p3 = b.vtt(sqb[:, 0:T], xsq[:, 0:T], xsB[:, 0:T], ALU.add,
                   deps=(p1, p2))
        b.wait(b.y, p3)
        e_out = b.dma(b.y, out_hbm[oc * 128:(oc + 1) * 128, :], sqb[:, 0:T])
        b.wait(b.v, Ev("ydma", e_out.val))
        return Ev("ydma", e_out.val)

    def gemm_head(w_hbm, noc, x_sb, nw_col, out_hbm, ev_nw, deps=()):
        ev_done = None
        for oc in range(noc):
            par = oc % 2
            wsl = wsl0 if par == 0 else wsl1
            b.wait(b.g, *wsl_last[par])
            ev_w = b.dma(b.g, wsl[:, 0:KC * 128].rearrange("p (k f) -> p k f", k=KC),
                         w_hbm[oc].rearrange("k p f -> p k f"), sem=f"gw{par}")
            evp = []
            for tbi in range(2):
                psd = ps_a if tbi == 0 else ps_b2
                b.wait(b.t, ev_w, ev_done, *deps)
                for kc in range(KC):
                    i = b.t.matmul(psd[:, 0:TB], wsl[:, kc * 128:(kc + 1) * 128],
                                   x_sb[:, kc * T + tbi * TB: kc * T + tbi * TB + TB],
                                   start=(kc == 0), stop=(kc == KC - 1))
                evp.append(b.mark(i, "mm"))
            wsl_last[par] = [evp[1]]
            ev_done = head_postproc(oc, (ps_a, ps_b2), evp, nw_col, out_hbm, ev_nw)
        return ev_done

    ev_q1 = gemm_head(qw1, KC, bb2, vcols[:, 0:1], q1_o, ev_qn, deps=(ev_hn2,))
    ev_k1 = gemm_head(kw1, 4, bb2, vcols[:, 1:2], k1_o, ev_kn, deps=(ev_hn2,))

    ev_use = None
    ev_wv_par = [None, None]
    for st in range(8):
        psd = ps_a if st % 2 == 0 else ps_b2
        for kg in range(4):
            par = kg % 2
            wsl = wsl0 if par == 0 else wsl1
            b.wait(b.g, *wsl_last[par])
            ev_w = b.dma(b.g, wsl[:, 0:4 * TB].rearrange("p (k f) -> p k f", k=4),
                         vw1r[0, kg * 4:(kg + 1) * 4].rearrange("k p f -> p k f"), sem=f"gw{par}")
            b.wait(b.t, ev_w, ev_hn2, ev_use)
            for kk in range(4):
                kc = kg * 4 + kk
                i = b.t.matmul(psd[:, 0:TB],
                               bb2[:, kc * T + st * 128: kc * T + st * 128 + 128],
                               wsl[:, kk * TB:(kk + 1) * TB],
                               start=(kc == 0), stop=(kc == KC - 1))
            ev_wv_par[par] = b.mark(i, "mm")
            wsl_last[par] = [ev_wv_par[par]]
        ev_mm = ev_wv_par[(4 - 1) % 2]
        e_cp = b.vcp(sqb[:, 0:TB], psd[:, 0:TB], deps=(ev_mm,))
        ev_use = e_cp
        b.wait(b.y, e_cp)
        e_st = b.dma(b.y, v1_o[st * 128:(st + 1) * 128, :], sqb[:, 0:TB])
        b.wait(b.v, Ev("ydma", e_st.val))

    b.wait(b.y, Ev("ydma", b.cnt["ydma"]))
    b.wait(b.g, Ev("gdma", b.cnt["gdma"]))
    b.close()
    return nc



def kernel(**ins):
    x_full = np.asarray(ins["L0_attn_hidden"])[0]
    rows = [core_rows(c) for c in range(NC)]

    wq0 = pack_lhsT(np.ascontiguousarray(ins["L0_attn_Wq"].T))
    wk0 = pack_lhsT(np.ascontiguousarray(ins["L0_attn_Wk"].T))
    wv0r = pack_rhs(np.ascontiguousarray(ins["L0_attn_Wv"].T), TB)
    wo0 = pack_lhsT(np.ascontiguousarray(ins["L0_attn_Wo"].T))
    qw1 = pack_lhsT(np.ascontiguousarray(ins["L0_ffn_q_w"].T))
    kw1 = pack_lhsT(np.ascontiguousarray(ins["L0_ffn_k_w"].T))
    vw1r = pack_rhs(np.ascontiguousarray(ins["L0_ffn_v_w"].T), TB)
    wmod0 = pack_rhs(np.ascontiguousarray(ins["L0_wmod"]), TB)
    memT = fm(np.asarray(ins["L0_attn_memory"]))

    common = dict(
        memT=memT, wq0=wq0, wk0=wk0, wv0r=wv0r, wo0=wo0, qw1=qw1, kw1=kw1,
        vw1r=vw1r, wmod0=wmod0,
        c0col=col16(np.asarray(ins["L0_c"]).reshape(-1)),
        gs_col=col16(np.asarray(ins["L0_attn_g_state"])),
        bs_col=col16(np.asarray(ins["L0_attn_b_state"])),
        gm_col=col16(np.asarray(ins["L0_attn_g_mem"])),
        bm_col=col16(np.asarray(ins["L0_attn_b_mem"])),
        log_beta=np.asarray(ins["L0_attn_log_beta"]).reshape(1, 8),
        anw_col=col16(np.asarray(ins["L0_ffn_attn_norm_w"])),
        qnw_col=np.asarray(ins["L0_ffn_q_norm_w"]).reshape(128, 1),
        knw_col=np.asarray(ins["L0_ffn_k_norm_w"]).reshape(128, 1),
    )
    in_maps = []
    for c in range(NC):
        m = dict(common)
        m["x0"] = fm(x_full[rows[c]])
        m["pos"] = rows[c].astype(np.float32).reshape(1, T)
        in_maps.append(m)

    ncA = build_launch_A()
    resA = run_bass_kernel_spmd(ncA, in_maps, core_ids=list(range(NC)))

    x1c = [np.asarray(resA.results[c]["x1"]) for c in range(NC)]
    h1c = [np.asarray(resA.results[c]["h1"]) for c in range(NC)]
    Q1c = [np.asarray(resA.results[c]["Q1"]) for c in range(NC)]
    K1c = [np.asarray(resA.results[c]["K1"]) for c in range(NC)]
    V1c = [np.asarray(resA.results[c]["V1"]) for c in range(NC)]
    mcolsA = np.asarray(resA.results[0]["mcolsA"])

    Kf1, Vt1 = _kv_pack(K1c, V1c, rows, 4)
    masks_pc = [_masks(c) for c in range(NC)]
    commonB = _host_common_B(ins)
    in_mapsB = []
    for c in range(NC):
        m = dict(commonB)
        m["x1"] = x1c[c]
        m["h1"] = h1c[c]
        m["Q1"] = Q1c[c]
        m["Kf"] = Kf1
        m["Vt"] = Vt1
        m["masks"] = masks_pc[c]
        m["mcolsA_i"] = mcolsA
        in_mapsB.append(m)
    ncB = build_launch_B()
    resB = run_bass_kernel_spmd(ncB, in_mapsB, core_ids=list(range(NC)))

    x2c = [np.asarray(resB.results[c]["x2"]) for c in range(NC)]
    Q2c = [np.asarray(resB.results[c]["Q2"]) for c in range(NC)]
    K2c = [np.asarray(resB.results[c]["K2"]) for c in range(NC)]
    V2c = [np.asarray(resB.results[c]["V2"]) for c in range(NC)]
    mcols1 = np.asarray(resB.results[0]["mcols1"])

    Kf2, Vt2 = _kv_pack(K2c, V2c, rows, 16)
    commonC = _host_common_C(ins)
    in_mapsC = []
    for c in range(NC):
        m = dict(commonC)
        m["x2"] = x2c[c]
        m["Q2"] = Q2c[c]
        m["Kf2"] = Kf2
        m["Vt2"] = Vt2
        m["masks"] = masks_pc[c]
        m["mcols1_i"] = mcols1
        m["pos"] = rows[c].astype(np.float32).reshape(1, T)
        in_mapsC.append(m)
    ncC = build_launch_C()
    resC = run_bass_kernel_spmd(ncC, in_mapsC, core_ids=list(range(NC)))

    x3c = [np.asarray(resC.results[c]["x3"]) for c in range(NC)]
    h3c = [np.asarray(resC.results[c]["h3"]) for c in range(NC)]
    Q3c = [np.asarray(resC.results[c]["Q3"]) for c in range(NC)]
    K3c = [np.asarray(resC.results[c]["K3"]) for c in range(NC)]
    V3c = [np.asarray(resC.results[c]["V3"]) for c in range(NC)]

    Kf3, Vt3 = _kv_pack(K3c, V3c, rows, 4)
    commonD = _host_common_D(ins)
    in_mapsD = []
    for c in range(NC):
        m = dict(commonD)
        m["x3"] = x3c[c]
        m["h3"] = h3c[c]
        m["Q3"] = Q3c[c]
        m["Kf3"] = Kf3
        m["Vt3"] = Vt3
        m["masks"] = masks_pc[c]
        m["mcols1_i"] = mcols1
        in_mapsD.append(m)
    ncD = build_launch_D()
    resD = run_bass_kernel_spmd(ncD, in_mapsD, core_ids=list(range(NC)))

    out = np.zeros((S, D), np.float32)
    for c in range(NC):
        out[rows[c]] = np.asarray(resD.results[c]["x4"]).T
    return out.reshape(1, S, D)


def _masks(c):
    mk = np.zeros((8, 128, 1024), np.float32)
    for m in range(8):
        G = zig_tiles(c)[m]
        qpos = 128 * G + np.arange(128)
        kpos = 128 * (8 * m + 8) - 1024 + np.arange(1024)
        mk[m] = (kpos[None, :] <= qpos[:, None])
    return mk.astype(BF)


def _kv_pack(Kc_list, Vc_list, rows, hkv):
    Kg = np.zeros((hkv * 128, S), BF)
    Vg = np.zeros((S, hkv * 128), BF)
    for c in range(NC):
        Kg[:, rows[c]] = Kc_list[c]
        Vg[rows[c]] = Vc_list[c]
    Kf = np.ascontiguousarray(Kg.reshape(hkv, 128, S))
    Vt = np.ascontiguousarray(Vg.reshape(64, 128, hkv, 128).transpose(2, 0, 1, 3))
    return Kf, Vt


def _host_common_B(ins):
    wqkvT = np.ascontiguousarray(np.asarray(ins["L1_attn_w_qkv"]).T)
    return dict(
        owT=pack_lhsT(np.ascontiguousarray(np.asarray(ins["L0_ffn_o_w"]).T)),
        fnw_col=col16(np.asarray(ins["L0_ffn_ffn_norm_w"])),
        gate_r=np.ascontiguousarray(np.asarray(ins["L0_ffn_gate_w"]).T).reshape(16, 128, 8).copy(),
        wg0=np.stack([pack_lhsT(np.ascontiguousarray(np.asarray(ins["L0_ffn_wg"][e]).T)) for e in range(8)]),
        wu0=np.stack([pack_lhsT(np.ascontiguousarray(np.asarray(ins["L0_ffn_wu"][e]).T)) for e in range(8)]),
        wd0=np.stack([np.ascontiguousarray(np.asarray(ins["L0_ffn_wd"][e]).T).reshape(11, 128, 2, 1024).transpose(2, 0, 1, 3).copy() for e in range(8)]),
        c1col=col16(np.asarray(ins["L1_c"]).reshape(-1)),
        wmod1=pack_rhs(np.ascontiguousarray(np.asarray(ins["L1_wmod"])), TB),
        wqkT=pack_lhsT(np.ascontiguousarray(wqkvT[:, 0:4096])),
        bqk_col=np.ascontiguousarray(np.asarray(ins["L1_attn_b_qkv"])[0:4096].reshape(32, 128).T),
        wv_r=pack_rhs(np.ascontiguousarray(wqkvT[:, 4096:6144]), TB),
        bv_row=np.asarray(ins["L1_attn_b_qkv"])[4096:6144].reshape(1, 2048),
    )


def _host_common_C(ins):
    colsin = np.zeros((128, 80), np.float32)
    colsin[:, 0:16] = col16(np.asarray(ins["L1_attn_b_out"]))
    colsin[:, 16:32] = col16(np.asarray(ins["L1_ffn_ln1_w"]))
    colsin[:, 32:48] = col16(np.asarray(ins["L1_ffn_qnorm_w"]))
    colsin[:, 48:52] = col16(np.asarray(ins["L1_ffn_knorm_w"]))
    return dict(
        woutT=pack_lhsT(np.ascontiguousarray(np.asarray(ins["L1_attn_w_out"]).T)),
        colsin=colsin,
        wqT=pack_lhsT(np.ascontiguousarray(np.asarray(ins["L1_ffn_wq"]))),
        wkT=pack_lhsT(np.ascontiguousarray(np.asarray(ins["L1_ffn_wk"]))),
        wv_r=pack_rhs(np.ascontiguousarray(np.asarray(ins["L1_ffn_wv"])), TB),
    )


def _host_common_D(ins):
    return dict(
        woT=pack_lhsT(np.ascontiguousarray(np.asarray(ins["L1_ffn_wo"]))),
        ln2_col=col16(np.asarray(ins["L1_ffn_ln2_w"])),
        router_r=np.ascontiguousarray(np.asarray(ins["L1_ffn_router_w"])).reshape(16, 128, 8).copy(),
        rbias_row=np.asarray(ins["L1_ffn_router_bias"]).reshape(1, 8).astype(np.float32),
        wgD=np.stack([pack_lhsT(np.ascontiguousarray(np.asarray(ins["L1_ffn_wg"][e]))) for e in range(8)]),
        wuD=np.stack([pack_lhsT(np.ascontiguousarray(np.asarray(ins["L1_ffn_wu"][e]))) for e in range(8)]),
        wdD=np.stack([np.ascontiguousarray(np.asarray(ins["L1_ffn_wd"][e])).reshape(8, 128, 2, 1024).transpose(2, 0, 1, 3).copy() for e in range(8)]),
    )


def run_launch_A(ins):
    x_full = np.asarray(ins["L0_attn_hidden"])[0]
    rows = [core_rows(c) for c in range(NC)]
    wq0 = pack_lhsT(np.ascontiguousarray(ins["L0_attn_Wq"].T))
    wk0 = pack_lhsT(np.ascontiguousarray(ins["L0_attn_Wk"].T))
    wv0r = pack_rhs(np.ascontiguousarray(ins["L0_attn_Wv"].T), TB)
    wo0 = pack_lhsT(np.ascontiguousarray(ins["L0_attn_Wo"].T))
    qw1 = pack_lhsT(np.ascontiguousarray(ins["L0_ffn_q_w"].T))
    kw1 = pack_lhsT(np.ascontiguousarray(ins["L0_ffn_k_w"].T))
    vw1r = pack_rhs(np.ascontiguousarray(ins["L0_ffn_v_w"].T), TB)
    wmod0 = pack_rhs(np.ascontiguousarray(ins["L0_wmod"]), TB)
    memT = fm(np.asarray(ins["L0_attn_memory"]))
    common = dict(
        memT=memT, wq0=wq0, wk0=wk0, wv0r=wv0r, wo0=wo0, qw1=qw1, kw1=kw1,
        vw1r=vw1r, wmod0=wmod0,
        c0col=col16(np.asarray(ins["L0_c"]).reshape(-1)),
        gs_col=col16(np.asarray(ins["L0_attn_g_state"])),
        bs_col=col16(np.asarray(ins["L0_attn_b_state"])),
        gm_col=col16(np.asarray(ins["L0_attn_g_mem"])),
        bm_col=col16(np.asarray(ins["L0_attn_b_mem"])),
        log_beta=np.asarray(ins["L0_attn_log_beta"]).reshape(1, 8),
        anw_col=col16(np.asarray(ins["L0_ffn_attn_norm_w"])),
        qnw_col=np.asarray(ins["L0_ffn_q_norm_w"]).reshape(128, 1),
        knw_col=np.asarray(ins["L0_ffn_k_norm_w"]).reshape(128, 1),
    )
    in_maps = []
    for c in range(NC):
        m = dict(common)
        m["x0"] = fm(x_full[rows[c]])
        m["pos"] = rows[c].astype(np.float32).reshape(1, T)
        in_maps.append(m)
    ncA = build_launch_A()
    return run_bass_kernel_spmd(ncA, in_maps, core_ids=list(range(NC))), rows



def causal_attention_masked(b, ten, Hq, kv_rep, scale, q_sb, kf_hbm, vt_hbm,
                            mmgr, out_sb, deps=(), dbg_ap=None):
    K_sb, V_sb = ten["K_sb"], ten["V_sb"]
    ps_sA, ps_sB = ten["ps_a"], ten["ps_b2"]
    ps_t, ps_pv = ten["ps_t"], ten["ps_pv"]
    ident = ten["ident"]
    rr = ten["rr"]
    pbf, ptb = ten["pbf"], ten["ptb"]
    identf = ten["identf"]
    nrow = ten["nrow"]
    ps_bc = ten["ps_bc"]
    onesf = ten["onesf"]
    ev_kcache = [None, None]
    last_evac = None
    last_tp = None
    for h in range(Hq):
        if h % kv_rep == 0:
            kvh = h // kv_rep
            b.wait(b.g, Ev("se", b.cnt["se"]), Ev("ve", b.cnt["ve"]),
                   Ev("mm", b.cnt["mm"]))
            e_k = b.dma(b.g, K_sb[:], kf_hbm[kvh], sem="gx0")
            e_v = b.dma(b.g, V_sb[:].rearrange("p (n f) -> p n f", n=64),
                        vt_hbm[kvh].rearrange("n p f -> p n f"), sem="gx1")
            ev_kcache = [e_k, e_v]
        for m in range(8):
            nb = 2 * m + 2
            mask_ap, ev_msk = mmgr.get(m)
            ev_red = None
            par_ev = [None, None]
            for nbk in range(nb):
                psd = ps_sA if nbk % 2 == 0 else ps_sB
                b.wait(b.t, ev_kcache[0], ev_kcache[1], par_ev[nbk % 2], *deps)
                i = b.t.matmul(psd[:, 0:TB],
                               q_sb[:, h * T + m * 128: h * T + m * 128 + 128],
                               K_sb[:, nbk * TB:(nbk + 1) * TB],
                               start=True, stop=True)
                e_mm = b.mark(i, "mm")
                ev_red = b.red(rr[:, nbk:nbk + 1], psd[:, 0:TB], ALU.max,
                               deps=(e_mm, ev_red))
                par_ev[nbk % 2] = ev_red
            e_m = b.red(rr[:, 32:33], rr[:, 0:nb], ALU.max, deps=(ev_red,))
            e_bn = b.vts(rr[:, 33:34], rr[:, 32:33], -scale, None, ALU.mult,
                         deps=(e_m,))
            ev_exp_prev = None
            ev_pv = None
            for nbk in range(nb):
                psd = ps_sA if nbk % 2 == 0 else ps_sB
                b.wait(b.t, e_bn, par_ev[nbk % 2], last_tp)
                i = b.t.matmul(psd[:, 0:TB],
                               q_sb[:, h * T + m * 128: h * T + m * 128 + 128],
                               K_sb[:, nbk * TB:(nbk + 1) * TB],
                               start=True, stop=True)
                e_mm = b.mark(i, "mm")
                mask_off = nbk * TB - (nb * TB - 1024)
                if mask_off >= 0:
                    e_p0 = b.act(pbf[:, 0:TB], psd[:, 0:TB], AF.Exp,
                                 bias=rr[:, 33:34], scale=scale,
                                 deps=(e_mm, ev_exp_prev, last_tp))
                    e_p = b.vstt(pbf[:, 0:TB], pbf[:, 0:TB], 1.0,
                                 mask_ap[:, mask_off:mask_off + TB],
                                 ALU.mult, ALU.mult,
                                 accum=rr[:, 16 + nbk:17 + nbk],
                                 deps=(e_p0, ev_msk))
                else:
                    e_p = b.act(pbf[:, 0:TB], psd[:, 0:TB], AF.Exp,
                                bias=rr[:, 33:34], scale=scale,
                                accum=rr[:, 16 + nbk:17 + nbk],
                                deps=(e_mm, ev_exp_prev, last_tp))
                ev_exp_prev = e_p
                par_ev[nbk % 2] = e_p
                if dbg_ap is not None and h == 0 and m == 0:
                    b.wait(b.g, e_p)
                    b.dma(b.g, dbg_ap[0:128, 5120 + nbk * 512: 5120 + (nbk + 1) * 512],
                          pbf[:, 0:TB])
                    b.wait(b.v, Ev("gdma", b.cnt["gdma"]))
                    b.wait(b.s, Ev("gdma", b.cnt["gdma"]))
                tp_e = None
                b.wait(b.t, e_p, last_evac)
                for j in range(4):
                    tp_e = b.tp(ps_t[:, j * 128:(j + 1) * 128],
                                pbf[:, j * 128:(j + 1) * 128], ident[:],
                                markit=(j == 3))
                e_cp = b.vcp(ptb[:, 0:TB], ps_t[:, 0:TB], deps=(tp_e,))
                last_tp = e_cp
                if dbg_ap is not None and h == 0 and m == 0 and nbk == 0:
                    b.wait(b.g, e_cp)
                    b.dma(b.g, dbg_ap[0:128, 6656:7168], ptb[:, 0:TB])
                    b.wait(b.t, Ev("gdma", b.cnt["gdma"]))
                    b.wait(b.v, Ev("gdma", b.cnt["gdma"]))
                b.wait(b.t, e_cp)
                for j in range(4):
                    kt = nbk * 4 + j
                    i = b.t.matmul(ps_pv[:, 0:128],
                                   V_sb[:, kt * 128:(kt + 1) * 128],
                                   ptb[:, j * 128:(j + 1) * 128],
                                   start=(nbk == 0 and j == 0),
                                   stop=(nbk == nb - 1 and j == 3))
                ev_pv = b.mark(i, "mm")
            if dbg_ap is not None and h == 0 and m == 0:
                e_t1 = b.vcp(ptb[:, 0:128], ps_pv[:, 0:128], deps=(ev_pv,))
                b.wait(b.g, e_t1)
                b.dma(b.g, dbg_ap[0:128, 7168:7296], ptb[:, 0:128])
                b.wait(b.t, Ev("gdma", b.cnt["gdma"]))
                b.wait(b.v, Ev("gdma", b.cnt["gdma"]))
            e_rs = b.red(rr[:, 34:35], rr[:, 16:16 + nb], ALU.add,
                         deps=(ev_exp_prev,))
            b.wait(b.v, e_rs)
            e_ri = b.mark(b.v.reciprocal(rr[:, 35:36], rr[:, 34:35]), "ve")
            b.wait(b.t, e_ri, ten.get("ev_idf"))
            e_tp2 = b.tp(ps_bc[0:1, 0:128], rr[:, 35:36], identf, markit=True)
            e_nr = b.vcp(nrow[0:1, 0:128], ps_bc[0:1, 0:128], deps=(e_tp2,))
            b.wait(b.t, e_nr)
            e_bc = b.mark(b.t.matmul(ps_bc[:, 128:256], onesf[0:1, :],
                                     nrow[0:1, 0:128], start=True, stop=True),
                          "mm")
            e_pvc = b.vcp(ptb[:, 0:128], ps_pv[:, 0:128], deps=(ev_pv,))
            e_out = b.vstt(out_sb[:, h * T + m * 128: h * T + m * 128 + 128],
                           ptb[:, 0:128], 1.0, ps_bc[:, 128:256],
                           ALU.mult, ALU.mult, deps=(e_bc, e_pvc))
            last_evac = e_out
            last_tp = e_out
            if dbg_ap is not None and h == 0 and m == 0:
                b.wait(b.g, e_out, e_ri)
                b.dma(b.g, dbg_ap[0:128, 6144:6184], rr[:, 0:40])
                b.wait(b.v, Ev("gdma", b.cnt["gdma"]))
    return last_evac


def rope_tables_cd(b, names, half, posv, omv, theta_scale, ropeC, ropeS, kint,
                   ropeT, xsA, xsB, ps_big, ev_deps):
    i_io = b.g.iota(omv[0:1, 0:128], pattern=[[1, 128]], base=0,
                    channel_multiplier=0, allow_small_or_imprecise_dtypes=True)
    ev_io = b.mark(i_io, "gc")
    ev_om = b.act(omv[0:1, 0:128], omv[0:1, 0:128], AF.Exp, scale=theta_scale,
                  deps=(ev_io,))
    b.wait(b.t, ev_om, *ev_deps)
    for tb0 in range(0, T, TB):
        e1 = b.t.matmul(ps_big[0:128, tb0:tb0 + TB], omv[0:1, 0:128],
                        posv[0:1, tb0:tb0 + TB], start=True, stop=True)
    ev_ang = b.mark(e1, "mm")
    evs = []
    prevw = None
    for which, bias in ((0, 0.0), (1, HALF_PI)):
        dst = ropeS if which == 0 else ropeC
        e2 = b.vts(xsB[0:half, 0:T], ps_big[0:half, 0:T], bias, None,
                   ALU.add, deps=(ev_ang, prevw))
        e3 = b.vts(xsA[0:half, 0:T], ps_big[0:half, 0:T], 1.0 / TWO_PI_HI,
                   bias / TWO_PI_HI, ALU.mult, ALU.add, deps=(ev_ang, prevw))
        e4 = b.vcp(kint[0:half, 0:T], xsA[0:half, 0:T], deps=(e3,))
        e5 = b.vcp(ropeT[0:half, 0:T], kint[0:half, 0:T], deps=(e4,))
        e5b = b.vtt(xsA[0:half, 0:T], xsA[0:half, 0:T], ropeT[0:half, 0:T],
                    ALU.subtract, deps=(e5,))
        e5c = b.vts(xsA[0:half, 0:T], xsA[0:half, 0:T], 0.5, None, ALU.is_ge,
                    deps=(e5b,))
        e5d = b.vtt(xsA[0:half, 0:T], ropeT[0:half, 0:T], xsA[0:half, 0:T],
                    ALU.add, deps=(e5c,))
        e6 = b.vstt(xsB[0:half, 0:T], xsA[0:half, 0:T], -TWO_PI_HI,
                    xsB[0:half, 0:T], ALU.mult, ALU.add, deps=(e5d, e2))
        e7 = b.vstt(xsB[0:half, 0:T], xsA[0:half, 0:T], -TWO_PI_LO,
                    xsB[0:half, 0:T], ALU.mult, ALU.add, deps=(e6,))
        e8 = b.vts(xsB[0:half, 0:T], xsB[0:half, 0:T], -PI_SAFE, PI_SAFE,
                   ALU.max, ALU.min, deps=(e7,))
        e9 = b.act(dst[0:half, 0:T], xsB[0:half, 0:T], AF.Sin, deps=(e8,))
        prevw = Ev("se", e9.val)
        evs.append(e9)
    evS, evC = evs
    b.wait(b.y, evC, evS)
    e_c2 = b.dma(b.y, ropeC[half:2 * half, 0:T], ropeC[0:half, 0:T])
    e_s2b = b.dma(b.y, ropeS[half:2 * half, 0:T], ropeS[0:half, 0:T])
    e_neg = b.vts(ropeS[0:half, 0:T], ropeS[0:half, 0:T], -1.0, None, ALU.mult,
                  deps=(Ev("ydma", e_s2b.val),))
    evl = [Ev("ve", e_neg.val), Ev("ydma", e_c2.val)]
    if 2 * half < 128:
        b.wait(b.g, *ev_deps)
        eC1 = b.mark(b.g.memset(ropeC[2 * half:128, 0:T], 1.0), "gc")
        eS0 = b.mark(b.g.memset(ropeS[2 * half:128, 0:T], 0.0), "gc")
        evl += [eC1, eS0]
    return evl


def zig_tiles_sym():
    return list(range(8))


def build_launch_B():
    nc = bass.Bass("TRN2", target_bir_lowering=False)
    b = Bld(nc)

    def din(name, shape, dt=F32):
        return nc.dram_tensor(name, list(shape), dt, kind="ExternalInput").ap()

    def dout(name, shape, dt=F32):
        return nc.dram_tensor(name, list(shape), dt, kind="ExternalOutput").ap()

    x1 = din("x1", [KC * 128, T])
    h1 = din("h1", [KC * 128, T])
    q1 = din("Q1", [KC * 128, T], BF16)
    kf = din("Kf", [4, 128, S], BF16)
    vt = din("Vt", [4, 64, 128, 128], BF16)
    masks = din("masks", [8, 128, 1024], BF16)
    owT = din("owT", [KC, KC, 128, 128])
    fnw_col = din("fnw_col", [128, KC])
    gate_r = din("gate_r", [KC, 128, 8])
    wg0 = din("wg0", [8, 11, KC, 128, 128])
    wu0 = din("wu0", [8, 11, KC, 128, 128])
    wd0 = din("wd0", [8, 2, 11, 128, 1024])
    mcolsA = din("mcolsA_i", [128, 6 * KC])
    c1col = din("c1col", [128, KC])
    wmod1 = din("wmod1", [24, KC, 128, TB])
    wqkT = din("wqkT", [32, KC, 128, 128])
    bqk_col = din("bqk_col", [128, 32])
    wv_r = din("wv_r", [4, KC, 128, TB])
    bv_row = din("bv_row", [1, 2048])

    x2_o = dout("x2", [KC * 128, T])
    mod1_o = dout("mod1", [1, 6 * D])
    mcols1_o = dout("mcols1", [128, 6 * KC])
    q2_o = dout("Q2", [KC * 128, T], BF16)
    k2_o = dout("K2", [KC * 128, T], BF16)
    v2_o = dout("V2", [T, 2048], BF16)
    dbgB = dout("dbgB", [128, 8192])

    modrow_h = nc.dram_tensor("modrow_h", [1, 6 * D], F32).ap()
    mcols_h = nc.dram_tensor("mcols_h", [6 * D], F32).ap()

    qat = b.sb("qat", [128, KC * T], BF16)
    K_sb = b.sb("K_sb", [128, S], BF16)
    V_sb = b.sb("V_sb", [128, S], BF16)
    hid = b.sb("hid", [128, KC * T], F32)
    h2b = b.sb("h2b", [128, KC * T], BF16)
    wsl0 = b.sb("wsl0", [128, 2048], BF16)
    wsl1 = b.sb("wsl1", [128, 2048], BF16)
    xsA = b.sb("xsA", [128, 1024], F32)
    xsB = b.sb("xsB", [128, 1024], F32)
    xsq = b.sb("xsq", [128, 1024], F32)
    sqb = b.sb("sqb", [128, 1024], BF16)
    pbf = b.sb("pbf", [128, 512], BF16)
    ptb = b.sb("ptb", [128, 512], BF16)
    mskc = b.sb("mskc", [128, 1024], BF16)
    rowsf = b.sb("rowsf", [128, 5 * 1024], F32)
    r_mean = rowsf[0:1, 0:1024]
    r_rstd = rowsf[0:1, 1024:2048]
    r_bt = rowsf[0:1, 2048:3072]
    r_scr = rowsf[0:1, 3072:4096]
    r_scr2 = rowsf[0:1, 4096:5120]
    crow0 = rowsf[0:1, 4096:5120]
    c_rows = rowsf[0:8, 0:1024]
    mrow = rowsf[0:1, 3072:3584]
    rr = b.sb("rr", [128, 48], F32)
    ident = b.sb("ident", [128, 128], BF16)
    onescol = b.sb("onescol", [128, 1], BF16)
    onesf = b.sb("onesf", [128, 128], F32)
    mcols = b.sb("mcols", [128, 6 * KC], F32)
    mcols1 = b.sb("mcols1_sb", [128, 6 * KC], F32)
    ccol = b.sb("ccol", [128, KC], F32)
    ccol_bf = sqb[:, 0:KC]
    vcols = b.sb("vcols", [128, KC], F32)

    c_b = xsA
    prb = xsB
    biasb = xsq

    ps_big = b.ps("ps_big", [128, 2048])
    ps_a = b.ps("ps_a", [128, 512])
    ps_b2 = b.ps("ps_b2", [128, 512])
    ps_t = b.ps("ps_t", [128, 1024], BF16)
    ps_pv = b.ps("ps_pv", [128, 128])

    wsl_last = [[], []]
    constt = rr[:, 40:48]
    _consts = {}

    def constf(val):
        val = float(np.float32(val))
        if val not in _consts:
            j = len(_consts)
            _consts[val] = j
            nc.const_aps.register(F32, val, constt[:, j:j + 1])
            b.mark(b.g.memset(constt[:, j:j + 1], val), "gc")
        return val

    ev_id = make_ident(b, ident[:])
    b.g.memset(onescol[:], 1.0)
    b.g.memset(rr[:], 0.0)
    b.mark(b.g.memset(onesf[:], 1.0), "gc")

    def mcol(vi, kc):
        return mcols[:, vi * KC + kc: vi * KC + kc + 1]

    def mcol1(vi, kc):
        return mcols1[:, vi * KC + kc: vi * KC + kc + 1]

    ev_mc = b.dma(b.g, mcols[:], mcolsA)
    ev_c = b.dma(b.g, ccol[:], c1col)
    ev_cb = b.act(ccol_bf[:], ccol[:], AF.Silu, deps=(ev_c,))
    ev_mm_par = [None, None]
    ev_row_prev = None
    for nb_ in range(24):
        for kg in range(4):
            par = kg % 2
            wsl = wsl0 if par == 0 else wsl1
            b.wait(b.g, ev_mm_par[par], *wsl_last[par])
            ev_w = b.dma(b.g, wsl[:, 0:4 * TB].rearrange("p (k f) -> p k f", k=4),
                         wmod1[nb_, kg * 4:(kg + 1) * 4].rearrange("k p f -> p k f"),
                         sem=f"gw{par}")
            b.wait(b.t, ev_w, ev_cb, ev_row_prev)
            for kk in range(4):
                kc = kg * 4 + kk
                i = b.t.matmul(ps_a[0:1, 0:TB], ccol_bf[:, kc:kc + 1],
                               wsl[:, kk * TB:(kk + 1) * TB],
                               start=(kc == 0), stop=(kc == KC - 1))
            ev_mm = b.mark(i, "mm")
            ev_mm_par[par] = ev_mm
            wsl_last[par] = [ev_mm]
        ev_row = b.vcp(mrow[0:1, 0:TB], ps_a[0:1, 0:TB], deps=(ev_mm,))
        b.wait(b.y, ev_row)
        ev_st = b.dma(b.y, modrow_h[0:1, nb_ * TB:(nb_ + 1) * TB], mrow[0:1, 0:TB])
        ev_st2 = b.dma(b.y, mod1_o[0:1, nb_ * TB:(nb_ + 1) * TB], mrow[0:1, 0:TB])
        ev_row_prev = Ev("ydma", max(ev_st.val, ev_st2.val))
    for vi in range(6):
        for half in range(2):
            hof = half * 1024
            ev_ld = b.dma(b.g, r_rstd[0:1, 0:1024],
                          modrow_h[0:1, vi * D + hof: vi * D + hof + 1024])
            srcr = r_rstd[0:1, 0:1024]
            if vi in (2, 5):
                e0 = b.act(r_bt[0:1, 0:1024], srcr, AF.Abs, deps=(ev_ld,))
                e1 = b.vts(r_bt[0:1, 0:1024], r_bt[0:1, 0:1024], 0.25, None,
                           ALU.max, deps=(e0,))
                e2 = b.act(r_scr[0:1, 0:1024], srcr, AF.Sign, deps=(ev_ld,))
                e3 = b.vtt(r_rstd[0:1, 0:1024], r_bt[0:1, 0:1024],
                           r_scr[0:1, 0:1024], ALU.mult, deps=(e1, e2))
                b.wait(b.y, e3)
            b.wait(b.y, Ev("gdma", ev_ld.val))
            ev_g = b.dma(b.y, mcols_h[vi * D + hof: vi * D + hof + 1024]
                         .rearrange("(o a) -> o a", o=1), r_rstd[0:1, 0:1024])
            b.wait(b.g, ev_g)
    for vi in range(6):
        for kc in range(KC):
            ev_mc1 = b.dma(b.g, mcols1[:, vi * KC + kc: vi * KC + kc + 1],
                           mcols_h[vi * D + kc * 128: vi * D + (kc + 1) * 128]
                           .rearrange("(p o) -> p o", o=1))
    b.wait(b.y, Ev("gdma", b.cnt["gdma"]))
    b.dma(b.y, mcols1_o, mcols1[:])

    ev_q = b.dma(b.g, qat[:].rearrange("p (k f) -> p k f", k=KC),
                 q1.rearrange("(k p) f -> p k f", p=128))

    mev = [None]

    class MaskMgr:
        def __init__(self):
            self.loaded = [None]

        def get(self, m):
            if self.loaded[0] == m:
                return mskc[:, 0:1024], mev[0]
            b.wait(b.g, mev[0], Ev("ve", b.cnt["ve"]))
            e = b.dma(b.g, mskc[:, 0:1024], masks[m])
            mev[0] = e
            self.loaded[0] = m
            return mskc[:, 0:1024], e

    mmgr = MaskMgr()

    identf = hid[:, 0:128]
    ev_idf = make_ident(b, identf)
    ten = dict(K_sb=K_sb, V_sb=V_sb, ps_a=ps_a, ps_b2=ps_b2, ps_t=ps_t,
               ps_pv=ps_pv, ident=ident, rr=rr, pbf=pbf, ptb=ptb,
               identf=identf, nrow=rowsf[0:1, 3584:3712],
               ps_bc=ps_big[:, 0:256], onesf=onesf, ev_idf=ev_idf)

    ev_attn = causal_attention_masked(b, ten, 16, 4, float(1.0 / np.sqrt(128.0)),
                                      qat, kf, vt, mmgr, qat,
                                      deps=(ev_q, ev_id), dbg_ap=dbgB)

    hstore = [None]
    hprev = [[], []]

    def ep_o(oc, tbi, psd, ev_mm):
        par = (oc * 2 + tbi) % 2
        buf = xsA if par == 0 else xsB
        b.wait(b.g, *hprev[par])
        e_x = b.dma(b.g, buf[:, 0:TB],
                    h1[oc * 128:(oc + 1) * 128, tbi * TB:(tbi + 1) * TB],
                    sem=f"gx{par}")
        e1 = b.vstt(hid[:, oc * T + tbi * TB: oc * T + tbi * TB + TB],
                    psd[:, 0:TB], 1.0, buf[:, 0:TB], ALU.mult, ALU.add,
                    deps=(ev_mm, e_x))
        hprev[par] = [e1]
        return e1

    def gemm(w_hbm, noc, nkc, x_sb, x_cols, epilogue, deps=()):
        ev_ep = None
        for oc in range(noc):
            par = oc % 2
            wsl = wsl0 if par == 0 else wsl1
            b.wait(b.g, *wsl_last[par])
            ev_w = b.dma(b.g, wsl[:, 0:nkc * 128].rearrange("p (k f) -> p k f", k=nkc),
                         w_hbm[oc].rearrange("k p f -> p k f"), sem=f"gw{par}")
            for tbi in range(x_cols // TB):
                psd = ps_a if tbi % 2 == 0 else ps_b2
                b.wait(b.t, ev_w, *deps)
                for kc in range(nkc):
                    i = b.t.matmul(psd[:, 0:TB], wsl[:, kc * 128:(kc + 1) * 128],
                                   x_sb[:, kc * x_cols + tbi * TB: kc * x_cols + tbi * TB + TB],
                                   start=(kc == 0), stop=(kc == nkc - 1))
                ev_mm = b.mark(i, "mm")
                wsl_last[par] = [ev_mm]
                ev_ep = epilogue(oc, tbi, psd, ev_mm)
                b.wait(b.t, ev_ep)
        return ev_ep

    b.wait(b.g, ev_attn)
    b.dma(b.g, dbgB[0:128, 0:2048], qat[:, 0:2048])
    ev_hid = gemm(owT, KC, KC, qat, T, ep_o, deps=(ev_attn, Ev("gdma", b.cnt["gdma"])))

    ev_fnw = b.dma(b.g, ccol[:], fnw_col)
    ev_sq_rd = None
    for kc in range(KC):
        sl = hid[:, kc * T:(kc + 1) * T]
        ev_sq = b.act(sqb[:, 0:T], sl, AF.Square, deps=(ev_hid, ev_sq_rd))
        b.wait(b.t, ev_sq, ev_hid)
        for tb0 in range(0, T, TB):
            i = b.t.matmul(ps_big[32:33, tb0:tb0 + TB], onescol[:, 0:1],
                           sqb[:, tb0:tb0 + TB], start=(kc == 0),
                           stop=(kc == KC - 1))
        ev_sq_rd = b.mark(i, "mm")
    ev_stat = ev_sq_rd
    e3 = b.vts(r_scr[0:1, 0:T], ps_big[32:33, 0:T], 1.0 / D, None, ALU.mult,
               deps=(ev_stat,))
    e4 = b.vts(r_rstd[0:1, 0:T], r_scr[0:1, 0:T], 1e-6, None, ALU.add, deps=(e3,))
    b.wait(b.v, e4)
    e5 = b.mark(b.v.reciprocal(r_scr[0:1, 0:T], r_rstd[0:1, 0:T]), "ve")
    e6 = b.act(r_rstd[0:1, 0:T], r_scr[0:1, 0:T], AF.Sqrt, deps=(e5,))
    ev_bcR = bcast_f32(b, ps_big[:, 0:T], onesf, r_rstd, T, deps=(e6, ev_stat))
    prev = None
    for kc in range(KC):
        e1_ = b.vstt(h2b[:, kc * T:(kc + 1) * T], hid[:, kc * T:(kc + 1) * T],
                     ccol[:, kc:kc + 1], ps_big[:, 0:T], ALU.mult, ALU.mult,
                     deps=(ev_bcR, ev_fnw, prev))
        prev = e1_
    ev_h2 = prev

    b.wait(b.g, ev_h2)
    b.dma(b.g, dbgB[0:128, 2048:3072], hid[:, 0:1024])
    b.dma(b.g, dbgB[0:128, 3072:4096], h2b[:, 0:1024])
    ev_gw = b.dma(b.g, wsl0[:, 0:KC * 8].rearrange("p (k f) -> p k f", k=KC),
                  gate_r.rearrange("k p f -> p k f"))
    wsl_last[0] = [Ev("mm", b.cnt["mm"])]
    ev_cr_list = []
    prev_tr = None
    for tt_ in range(8):
        b.wait(b.t, ev_gw, ev_h2, prev_tr)
        for kc in range(KC):
            i = b.t.matmul(ps_a[0:128, 0:8],
                           h2b[:, kc * T + tt_ * 128: kc * T + tt_ * 128 + 128],
                           wsl0[:, kc * 8:(kc + 1) * 8],
                           start=(kc == 0), stop=(kc == KC - 1))
        e_mm = b.mark(i, "mm")
        c0 = tt_ * 6
        e_m = b.red(prb[:, c0:c0 + 1], ps_a[0:128, 0:8], ALU.max, deps=(e_mm,))
        e_bn = b.vts(prb[:, c0 + 1:c0 + 2], prb[:, c0:c0 + 1], -1.0, None,
                     ALU.mult, deps=(e_m,))
        e_e = b.act(prb[:, 40 + 0:40 + 8] if False else c_b[:, 0:8], ps_a[0:128, 0:8],
                    AF.Exp, bias=prb[:, c0 + 1:c0 + 2],
                    accum=prb[:, c0 + 2:c0 + 3], deps=(e_bn,))
        b.wait(b.v, e_e)
        e_ri = b.mark(b.v.reciprocal(prb[:, c0 + 3:c0 + 4], prb[:, c0 + 2:c0 + 3]), "ve")
        e_pr = b.vts(c_b[:, 8:16], c_b[:, 0:8], prb[:, c0 + 3:c0 + 4], None,
                     ALU.mult, deps=(e_ri,))
        e_m1 = b.red(prb[:, c0 + 4:c0 + 5], c_b[:, 8:16], ALU.max, deps=(e_pr,))
        e_eq = b.vts(c_b[:, 16:24], c_b[:, 8:16], prb[:, c0 + 4:c0 + 5], None,
                     ALU.is_equal, deps=(e_m1,))
        e_msk = b.vstt(c_b[:, 24:32], c_b[:, 16:24], -1e30, c_b[:, 8:16],
                       ALU.mult, ALU.add, deps=(e_eq,))
        e_m2 = b.red(prb[:, c0 + 5:c0 + 6], c_b[:, 24:32], ALU.max, deps=(e_msk,))
        e_sel = b.vts(c_b[:, 16:24], c_b[:, 8:16], prb[:, c0 + 5:c0 + 6], None,
                      ALU.is_ge, deps=(e_m2,))
        e_w = b.vtt(c_b[:, 24:32], c_b[:, 8:16], c_b[:, 16:24], ALU.mult,
                    deps=(e_sel,))
        e_sw = b.red(prb[:, c0 + 4:c0 + 5], c_b[:, 24:32], ALU.add, deps=(e_w,))
        e_eps = b.vts(prb[:, c0 + 5:c0 + 6], prb[:, c0 + 4:c0 + 5], 1e-9, None,
                      ALU.add, deps=(e_sw,))
        b.wait(b.v, e_eps)
        e_rd = b.mark(b.v.reciprocal(prb[:, c0 + 4:c0 + 5], prb[:, c0 + 5:c0 + 6]), "ve")
        e_cmb = b.vts(c_b[:, 32:40], c_b[:, 24:32], prb[:, c0 + 4:c0 + 5], None,
                      ALU.mult, deps=(e_rd,))
        e_cb16 = b.vcp(sqb[:, 0:8], c_b[:, 32:40], deps=(e_cmb,))
        b.wait(b.t, e_cb16, ev_id)
        tp_e = b.tp(ps_t[0:8, 0:128], sqb[:, 0:8], ident[:], markit=True)
        e_cp = b.vcp(c_rows[0:8, tt_ * 128:(tt_ + 1) * 128], ps_t[0:8, 0:128],
                     deps=(tp_e,))
        prev_tr = e_cp
        ev_cr_list.append(e_cp)
    wsl_last[0] = [Ev("mm", b.cnt["mm"])]
    b.wait(b.g, *ev_cr_list)
    b.dma(b.g, dbgB[0:8, 4096:5120], c_rows[0:8, 0:1024])
    b.wait(b.v, Ev("gdma", b.cnt["gdma"]))

    act_v = qat
    ev_moe_prev = None
    for e_ in range(8):
        b.wait(b.y, *ev_cr_list, ev_moe_prev)
        e_c0 = b.dma(b.y, crow0[0:1, :], c_rows[e_:e_ + 1, :])
        b.wait(b.t, Ev("ydma", b.cnt["ydma"]))
        e_cb2 = bcast_f32(b, ps_big[:, 1024:2048], onesf, crow0, 1024)
        e_cbs = b.vcp(c_b[:, 0:1024], ps_big[:, 1024:2048], deps=(e_cb2,))
        ev_act = []
        for ic in range(11):
            b.wait(b.g, *wsl_last[0])
            ev_wg = b.dma(b.g, wsl0[:, 0:KC * 128].rearrange("p (k f) -> p k f", k=KC),
                          wg0[e_, ic].rearrange("k p f -> p k f"), sem="gw0")
            b.wait(b.g, *wsl_last[1])
            ev_wu = b.dma(b.g, wsl1[:, 0:KC * 128].rearrange("p (k f) -> p k f", k=KC),
                          wu0[e_, ic].rearrange("k p f -> p k f"), sem="gw1")
            for tbi in range(2):
                b.wait(b.t, ev_wg, ev_h2, ev_moe_prev)
                for kc in range(KC):
                    i = b.t.matmul(ps_a[:, 0:TB], wsl0[:, kc * 128:(kc + 1) * 128],
                                   h2b[:, kc * T + tbi * TB: kc * T + tbi * TB + TB],
                                   start=(kc == 0), stop=(kc == KC - 1))
                e_g = b.mark(i, "mm")
                b.wait(b.t, ev_wu)
                for kc in range(KC):
                    i = b.t.matmul(ps_b2[:, 0:TB], wsl1[:, kc * 128:(kc + 1) * 128],
                                   h2b[:, kc * T + tbi * TB: kc * T + tbi * TB + TB],
                                   start=(kc == 0), stop=(kc == KC - 1))
                e_u = b.mark(i, "mm")
                e_sg = b.act(sqb[:, 0:TB], ps_a[:, 0:TB], AF.Silu, deps=(e_g,))
                act_sl = act_v[:, ic * T + tbi * TB: ic * T + tbi * TB + TB]
                e_a1 = b.vstt(act_sl, sqb[:, 0:TB], 1.0, ps_b2[:, 0:TB],
                              ALU.mult, ALU.mult, deps=(e_sg, e_u, e_cbs))
                e_a2 = b.vstt(act_sl, act_sl, 1.0,
                              c_b[:, tbi * TB: tbi * TB + TB],
                              ALU.mult, ALU.mult, deps=(e_a1,))
                ev_act.append(e_a2)
                b.wait(b.t, e_sg, e_a1, e_a2)
            wsl_last[0] = [e_g]
            wsl_last[1] = [e_u]
        ev_dn = None
        for och in range(2):
            wd_ev = []
            b.wait(b.g, Ev("mm", b.cnt["mm"]))
            for ic in range(11):
                dst = K_sb[:, ic * 1024:(ic + 1) * 1024] if ic < 8 else \
                    V_sb[:, (ic - 8) * 1024:(ic - 7) * 1024]
                e_wd = b.dma(b.g, dst, wd0[e_, och, ic], sem="gx0")
            wd_ev = Ev("gx0", b.cnt["gx0"])
            for oc_i in range(8):
                oc = och * 8 + oc_i
                for tbi in range(2):
                    psd = ps_a if (oc_i * 2 + tbi) % 2 == 0 else ps_b2
                    b.wait(b.t, wd_ev, *ev_act, ev_dn)
                    for ic in range(11):
                        lh = K_sb[:, ic * 1024 + oc_i * 128: ic * 1024 + oc_i * 128 + 128] \
                            if ic < 8 else \
                            V_sb[:, (ic - 8) * 1024 + oc_i * 128: (ic - 8) * 1024 + oc_i * 128 + 128]
                        i = b.t.matmul(psd[:, 0:TB], lh,
                                       act_v[:, ic * T + tbi * TB: ic * T + tbi * TB + TB],
                                       start=(ic == 0), stop=(ic == 10))
                    e_d = b.mark(i, "mm")
                    e_f = b.vstt(hid[:, oc * T + tbi * TB: oc * T + tbi * TB + TB],
                                 psd[:, 0:TB], 1.0,
                                 hid[:, oc * T + tbi * TB: oc * T + tbi * TB + TB],
                                 ALU.mult, ALU.add, deps=(e_d,))
                    ev_dn = e_f
        ev_moe_prev = ev_dn

    x2prev = [[], []]
    ev_x2 = None
    for kc in range(KC):
        for tbi in range(2):
            par = (kc * 2 + tbi) % 2
            buf = xsA if par == 0 else xsB
            b.wait(b.g, *x2prev[par])
            e_x = b.dma(b.g, buf[:, 0:TB],
                        x1[kc * 128:(kc + 1) * 128, tbi * TB:(tbi + 1) * TB],
                        sem=f"gx{par}")
            e1 = b.vstt(xsq[:, 0:TB], hid[:, kc * T + tbi * TB: kc * T + tbi * TB + TB],
                        mcol(5, kc), buf[:, 0:TB], ALU.mult, ALU.add,
                        deps=(ev_moe_prev, e_x, ev_mc, ev_x2))
            x2prev[par] = [e1]
            b.wait(b.y, e1)
            e_st = b.dma(b.y, x2_o[kc * 128:(kc + 1) * 128, tbi * TB:(tbi + 1) * TB],
                         xsq[:, 0:TB])
            ev_x2 = Ev("ydma", e_st.val)

    xev = [[], []]
    ev_sq_rd = None
    ev_mean_rd = None
    for kc in range(KC):
        par = kc % 2
        buf = xsA if par == 0 else xsB
        b.wait(b.g, *xev[par], ev_x2)
        e_x = b.dma(b.g, buf[:, 0:T],
                    x2_o[kc * 128:(kc + 1) * 128, :], sem=f"gx{par}")
        ev_sq = b.act(sqb[:, 0:T], buf[:, 0:T], AF.Square, deps=(e_x, ev_sq_rd))
        ev_cp = b.vcp(pbf[:, 0:TB], buf[:, 0:TB], deps=(e_x, ev_mean_rd))
        ev_cp2 = b.vcp(ptb[:, 0:TB], buf[:, TB:T], deps=(e_x, ev_mean_rd))
        b.wait(b.t, ev_sq, ev_cp, ev_cp2)
        b.mm(ps_big[0:1, 0:TB], onescol[:, 0:1], pbf[:, 0:TB],
             start=(kc == 0), stop=(kc == KC - 1))
        b.mm(ps_big[0:1, TB:T], onescol[:, 0:1], ptb[:, 0:TB],
             start=(kc == 0), stop=(kc == KC - 1))
        for tb0 in range(0, T, TB):
            i = b.t.matmul(ps_big[32:33, tb0:tb0 + TB], onescol[:, 0:1],
                           sqb[:, tb0:tb0 + TB], start=(kc == 0),
                           stop=(kc == KC - 1))
        ev_last = b.mark(i, "mm")
        ev_sq_rd = ev_last
        ev_mean_rd = ev_last
        xev[par] = [ev_sq, ev_cp, ev_cp2]
    ev_stat = ev_last
    e0 = b.vts(r_mean[0:1, 0:T], ps_big[0:1, 0:T], 1.0 / D, None, ALU.mult,
               deps=(ev_stat,))
    e1 = b.vts(r_scr[0:1, 0:T], ps_big[32:33, 0:T], 1.0 / D, None, ALU.mult,
               deps=(ev_stat,))
    e2 = b.vtt(r_rstd[0:1, 0:T], r_mean[0:1, 0:T], r_mean[0:1, 0:T], ALU.mult,
               deps=(e0,))
    e3 = b.vtt(r_scr[0:1, 0:T], r_scr[0:1, 0:T], r_rstd[0:1, 0:T], ALU.subtract,
               deps=(e1, e2))
    e4 = b.vts(r_rstd[0:1, 0:T], r_scr[0:1, 0:T], 1e-5, None, ALU.add, deps=(e3,))
    b.wait(b.v, e4)
    e5 = b.mark(b.v.reciprocal(r_scr[0:1, 0:T], r_rstd[0:1, 0:T]), "ve")
    e6 = b.act(r_rstd[0:1, 0:T], r_scr[0:1, 0:T], AF.Sqrt, deps=(e5,))
    e7 = b.vstt(r_bt[0:1, 0:T], r_mean[0:1, 0:T], -1.0, r_rstd[0:1, 0:T],
                ALU.mult, ALU.mult, deps=(e6,))
    ev_bcR = bcast_f32(b, ps_big[:, 0:T], onesf, r_rstd, T, deps=(e7, ev_stat))
    ev_bcB = bcast_f32(b, ps_big[:, T:2 * T], onesf, r_bt, T, deps=(e7,))
    xev2 = [list(xev[0]), list(xev[1])]
    hb_ev = [None, None]
    for kc in range(KC):
        par = kc % 2
        buf = xsA if par == 0 else xsB
        obuf = xsB if par == 0 else xsA
        b.wait(b.g, *xev2[par], hb_ev[1 - par])
        e_x = b.dma(b.g, buf[:, 0:T], x2_o[kc * 128:(kc + 1) * 128, :],
                    sem=f"gx{par}")
        e1_ = b.vtt(xsq[:, 0:T], buf[:, 0:T], ps_big[:, 0:T], ALU.mult,
                    deps=(e_x, ev_bcR, hb_ev[1 - par]))
        e2_ = b.vtt(obuf[:, 0:T], xsq[:, 0:T], ps_big[:, T:2 * T], ALU.add,
                    deps=(e1_, ev_bcB))
        e3_ = b.vts(vcols[:, kc:kc + 1], mcol1(1, kc), 1.0, None, ALU.add,
                    deps=(Ev("gdma", b.cnt["gdma"]),))
        e4_ = b.vts(h2b[:, kc * T:(kc + 1) * T], obuf[:, 0:T],
                    vcols[:, kc:kc + 1], mcol1(0, kc), ALU.mult, ALU.add,
                    deps=(e2_, e3_))
        xev2[par] = [e1_]
        hb_ev[par] = e4_
    ev_hB = Ev("ve", b.cnt["ve"])

    b.wait(b.g, Ev("ve", b.cnt["ve"]), Ev("se", b.cnt["se"]), ev_hB)
    ev_bq = b.dma(b.g, prb[:, 0:32], bqk_col)

    def ep_qk(oc, tbi, psd, ev_mm):
        dsto = q2_o if oc < 16 else k2_o
        occ = oc % 16
        e1 = b.vts(sqb[:, 0:TB], psd[:, 0:TB], prb[:, oc:oc + 1], None, ALU.add,
                   deps=(ev_mm, ev_bq))
        b.wait(b.y, e1)
        e_st = b.dma(b.y, dsto[occ * 128:(occ + 1) * 128, tbi * TB:(tbi + 1) * TB],
                     sqb[:, 0:TB])
        b.wait(b.v, Ev("ydma", e_st.val))
        return e1

    gemm(wqkT, 32, KC, h2b, T, ep_qk, deps=(ev_hB,))

    for vb in range(4):
        b.wait(b.y, Ev("ve", b.cnt["ve"]))
        e_bvl = b.dma(b.g, crow0[0:1, 0:512], bv_row[0:1, vb * 512:(vb + 1) * 512])
        b.wait(b.t, e_bvl, Ev("ve", b.cnt["ve"]), Ev("mm", b.cnt["mm"]))
        e_bb = bcast_f32(b, ps_a[:, 0:512], onesf, crow0, 512)
        e_bc = b.vcp(biasb[:, 0:512], ps_a[:, 0:512], deps=(e_bb,))
        ev_bb_done = e_bc
        b.wait(b.t, e_bc)
        for stg in range(2):
            ev_kg = None
            for kg in range(4):
                par = kg % 2
                wsl = wsl0 if par == 0 else wsl1
                b.wait(b.g, *wsl_last[par])
                ev_w = b.dma(b.g, wsl[:, 0:2048].rearrange("p (k f) -> p k f", k=4),
                             wv_r[vb, kg * 4:(kg + 1) * 4].rearrange("k p f -> p k f"),
                             sem=f"gw{par}")
                b.wait(b.t, ev_w, ev_hB, ev_bb_done)
                for st_i in range(4):
                    st = stg * 4 + st_i
                    for kk in range(4):
                        kc = kg * 4 + kk
                        i2 = b.t.matmul(ps_big[:, st_i * TB:(st_i + 1) * TB],
                                        h2b[:, kc * T + st * 128: kc * T + st * 128 + 128],
                                        wsl[:, kk * TB:(kk + 1) * TB],
                                        start=(kc == 0), stop=(kc == KC - 1))
                e_mm = b.mark(i2, "mm")
                wsl_last[par] = [e_mm]
                ev_kg = e_mm
            for st_i in range(4):
                st = stg * 4 + st_i
                e_v = b.vstt(sqb[:, 0:TB], ps_big[:, st_i * TB:(st_i + 1) * TB],
                             1.0, biasb[:, 0:TB],
                             ALU.mult, ALU.add, deps=(ev_kg, ev_bb_done))
                b.wait(b.y, e_v)
                e_st = b.dma(b.y, v2_o[st * 128:(st + 1) * 128, vb * TB:(vb + 1) * TB],
                             sqb[:, 0:TB])
                b.wait(b.v, Ev("ydma", e_st.val))
            b.wait(b.t, Ev("ve", b.cnt["ve"]))

    b.wait(b.y, Ev("ydma", b.cnt["ydma"]))
    b.wait(b.g, Ev("gdma", b.cnt["gdma"]))
    b.close()
    return nc



def build_launch_C(cut=99):
    nc = bass.Bass("TRN2", target_bir_lowering=False)
    b = Bld(nc)

    def din(name, shape, dt=F32):
        return nc.dram_tensor(name, list(shape), dt, kind="ExternalInput").ap()

    def dout(name, shape, dt=F32):
        return nc.dram_tensor(name, list(shape), dt, kind="ExternalOutput").ap()

    x2 = din("x2", [KC * 128, T])
    q2 = din("Q2", [KC * 128, T], BF16)
    kf = din("Kf2", [16, 128, S], BF16)
    vt = din("Vt2", [16, 64, 128, 128], BF16)
    masks = din("masks", [8, 128, 1024], BF16)
    woutT = din("woutT", [KC, KC, 128, 128])
    mcols1_i = din("mcols1_i", [128, 6 * KC])
    colsin = din("colsin", [128, 80])
    wqT = din("wqT", [KC, KC, 128, 128])
    wkT = din("wkT", [4, KC, 128, 128])
    wv_r = din("wv_r", [1, KC, 128, TB])
    pos = din("pos", [1, T])

    x3_o = dout("x3", [KC * 128, T])
    h3_o = dout("h3", [KC * 128, T])
    q3_o = dout("Q3", [KC * 128, T], BF16)
    k3_o = dout("K3", [4 * 128, T], BF16)
    v3_o = dout("V3", [T, 512], BF16)
    dbgC = dout("dbgC", [128, 8192])

    qat = b.sb("qat", [128, KC * T], BF16)
    K_sb = b.sb("K_sb", [128, S], BF16)
    V_sb = b.sb("V_sb", [128, S], BF16)
    hid = b.sb("hid", [128, KC * T], F32)
    h2b = b.sb("h2b", [128, KC * T], BF16)
    wsl0 = b.sb("wsl0", [128, 2048], BF16)
    wsl1 = b.sb("wsl1", [128, 2048], BF16)
    xsA = b.sb("xsA", [128, 1024], F32)
    xsB = b.sb("xsB", [128, 1024], F32)
    xsq = b.sb("xsq", [128, 1024], F32)
    sqb = b.sb("sqb", [128, 1024], BF16)
    pbf = b.sb("pbf", [128, 512], BF16)
    ptb = b.sb("ptb", [128, 512], BF16)
    mskc = b.sb("mskc", [128, 1024], BF16)
    rowsf = b.sb("rowsf", [128, 4 * 1024], F32)
    rr = b.sb("rr", [128, 48], F32)
    ident = b.sb("ident", [128, 128], BF16)
    onescol = b.sb("onescol", [128, 1], BF16)
    onesf = b.sb("onesf", [128, 128], F32)
    mcols1 = b.sb("mcols1_sb", [128, 6 * KC], F32)
    cols = b.sb("cols", [128, 80], F32)
    kint = b.sb("kint", [128, T], mybir.dt.int32)
    ropeT = xsq
    ropeC = K_sb[:, 0:T]
    ropeS = K_sb[:, T:2 * T]

    r_mean = rowsf[0:1, 0:1024]
    r_rstd = rowsf[0:1, 1024:2048]
    r_bt = rowsf[0:1, 2048:3072]
    r_scr = rowsf[0:1, 3072:4096]
    posv = rowsf[0:1, 3072:4096]
    omv = rowsf[0:1, 2944:3072]

    ps_big = b.ps("ps_big", [128, 2048])
    ps_a = b.ps("ps_a", [128, 512])
    ps_b2 = b.ps("ps_b2", [128, 512])
    ps_t = b.ps("ps_t", [128, 1024], BF16)
    ps_pv = b.ps("ps_pv", [128, 128])

    wsl_last = [[], []]

    ev_id = make_ident(b, ident[:])
    b.g.memset(onescol[:], 1.0)
    b.g.memset(rr[:], 0.0)
    b.mark(b.g.memset(onesf[:], 1.0), "gc")

    ev_mc1 = b.dma(b.g, mcols1[:], mcols1_i)
    ev_cols = b.dma(b.g, cols[:, 0:80], colsin)
    ev_pos = b.dma(b.g, posv, pos)

    def mcol1(vi, kc):
        return mcols1[:, vi * KC + kc: vi * KC + kc + 1]

    ev_q = b.dma(b.g, qat[:].rearrange("p (k f) -> p k f", k=KC),
                 q2.rearrange("(k p) f -> p k f", p=128))

    mev = [None]

    class MaskMgr:
        def __init__(self):
            self.loaded = [None]

        def get(self, m):
            if self.loaded[0] == m:
                return mskc[:, 0:1024], mev[0]
            b.wait(b.g, mev[0], Ev("ve", b.cnt["ve"]))
            e = b.dma(b.g, mskc[:, 0:1024], masks[m])
            mev[0] = e
            self.loaded[0] = m
            return mskc[:, 0:1024], e

    mmgr = MaskMgr()
    identf = hid[:, 0:128]
    ev_idf = make_ident(b, identf)
    ten = dict(K_sb=K_sb, V_sb=V_sb, ps_a=ps_a, ps_b2=ps_b2, ps_t=ps_t,
               ps_pv=ps_pv, ident=ident, rr=rr, pbf=pbf, ptb=ptb,
               identf=identf, nrow=rowsf[0:1, 2048:2176],
               ps_bc=ps_big[:, 0:256], onesf=onesf, ev_idf=ev_idf)

    scl = float(1.0 / np.sqrt(128.0))
    ev_attn = causal_attention_masked(b, ten, 16, 1, scl, qat, kf, vt, mmgr,
                                      qat, deps=(ev_q, ev_id))
    b.wait(b.g, ev_attn)
    b.dma(b.g, dbgC[0:128, 0:2048], qat[:, 0:2048])
    if cut <= 1:
        b.wait(b.y, Ev("ydma", b.cnt["ydma"]))
        b.wait(b.g, Ev("gdma", b.cnt["gdma"]))
        b.close()
        return nc

    evtab = rope_tables_cd(b, None, 32, posv, omv,
                           float(-np.log(1e6) / 32.0), ropeC, ropeS, kint,
                           ropeT, xsA, xsB, ps_big,
                           (ev_attn, ev_pos, Ev("gdma", b.cnt["gdma"]),
                            Ev("ve", b.cnt["ve"]), Ev("se", b.cnt["se"])))

    if cut <= 2:
        b.wait(b.y, Ev("ydma", b.cnt["ydma"]))
        b.wait(b.g, Ev("gdma", b.cnt["gdma"]))
        b.close()
        return nc

    hprev = [[], []]
    epo_prev = [None]

    def ep_oC(oc, tbi, psd, ev_mm):
        par = (oc * 2 + tbi) % 2
        buf = xsA if par == 0 else xsB
        b.wait(b.g, *hprev[par])
        e_x = b.dma(b.g, buf[:, 0:TB],
                    x2[oc * 128:(oc + 1) * 128, tbi * TB:(tbi + 1) * TB],
                    sem=f"gx{par}")
        e_a = b.vts(xsq[:, 0:TB], psd[:, 0:TB], cols[:, oc:oc + 1],
                    mcol1(2, oc), ALU.add, ALU.mult,
                    deps=(ev_mm, ev_cols, ev_mc1, epo_prev[0]))
        e1 = b.vtt(hid[:, oc * T + tbi * TB: oc * T + tbi * TB + TB],
                   xsq[:, 0:TB], buf[:, 0:TB], ALU.add, deps=(e_a, e_x))
        epo_prev[0] = e1
        hprev[par] = [e1]
        b.wait(b.y, e1)
        b.dma(b.y, x3_o[oc * 128:(oc + 1) * 128, tbi * TB:(tbi + 1) * TB],
              hid[:, oc * T + tbi * TB: oc * T + tbi * TB + TB])
        return e1

    def gemm(w_hbm, noc, nkc, x_sb, x_cols, epilogue, deps=()):
        ev_ep = None
        for oc in range(noc):
            par = oc % 2
            wsl = wsl0 if par == 0 else wsl1
            b.wait(b.g, *wsl_last[par])
            ev_w = b.dma(b.g, wsl[:, 0:nkc * 128].rearrange("p (k f) -> p k f", k=nkc),
                         w_hbm[oc].rearrange("k p f -> p k f"), sem=f"gw{par}")
            for tbi in range(x_cols // TB):
                psd = ps_a if tbi % 2 == 0 else ps_b2
                b.wait(b.t, ev_w, *deps)
                for kc in range(nkc):
                    i = b.t.matmul(psd[:, 0:TB], wsl[:, kc * 128:(kc + 1) * 128],
                                   x_sb[:, kc * x_cols + tbi * TB: kc * x_cols + tbi * TB + TB],
                                   start=(kc == 0), stop=(kc == nkc - 1))
                ev_mm = b.mark(i, "mm")
                wsl_last[par] = [ev_mm]
                ev_ep = epilogue(oc, tbi, psd, ev_mm)
                b.wait(b.t, ev_ep)
        return ev_ep

    b.wait(b.g, Ev("ve", b.cnt["ve"]), Ev("se", b.cnt["se"]), *evtab)
    ev_hid = gemm(woutT, KC, KC, qat, T, ep_oC,
                  deps=(ev_attn, *evtab, Ev("se", b.cnt["se"]),
                        Ev("ve", b.cnt["ve"]), Ev("gdma", b.cnt["gdma"])))
    b.wait(b.g, ev_hid)
    b.dma(b.g, dbgC[0:128, 2048:3072], hid[:, 0:1024])
    if cut <= 3:
        b.wait(b.y, Ev("ydma", b.cnt["ydma"]))
        b.wait(b.g, Ev("gdma", b.cnt["gdma"]))
        b.close()
        return nc

    ev_sq_rd = None
    for kc in range(KC):
        sl = hid[:, kc * T:(kc + 1) * T]
        ev_sq = b.act(sqb[:, 0:T], sl, AF.Square, deps=(ev_hid, ev_sq_rd))
        ev_cp = b.vcp(pbf[:, 0:TB], sl[:, 0:TB], deps=(ev_hid, ev_sq_rd))
        ev_cp2 = b.vcp(ptb[:, 0:TB], sl[:, TB:T], deps=(ev_hid, ev_sq_rd))
        b.wait(b.t, ev_sq, ev_cp, ev_cp2)
        b.mm(ps_big[0:1, 0:TB], onescol[:, 0:1], pbf[:, 0:TB],
             start=(kc == 0), stop=(kc == KC - 1))
        b.mm(ps_big[0:1, TB:T], onescol[:, 0:1], ptb[:, 0:TB],
             start=(kc == 0), stop=(kc == KC - 1))
        for tb0 in range(0, T, TB):
            i = b.t.matmul(ps_big[32:33, tb0:tb0 + TB], onescol[:, 0:1],
                           sqb[:, tb0:tb0 + TB], start=(kc == 0),
                           stop=(kc == KC - 1))
        ev_sq_rd = b.mark(i, "mm")
    ev_stat = ev_sq_rd
    e0 = b.vts(r_mean[0:1, 0:T], ps_big[0:1, 0:T], 1.0 / D, None, ALU.mult,
               deps=(ev_stat,))
    e1 = b.vts(r_scr[0:1, 0:T], ps_big[32:33, 0:T], 1.0 / D, None, ALU.mult,
               deps=(ev_stat,))
    e2 = b.vtt(r_rstd[0:1, 0:T], r_mean[0:1, 0:T], r_mean[0:1, 0:T], ALU.mult,
               deps=(e0,))
    e3 = b.vtt(r_scr[0:1, 0:T], r_scr[0:1, 0:T], r_rstd[0:1, 0:T],
               ALU.subtract, deps=(e1, e2))
    e4 = b.vts(r_rstd[0:1, 0:T], r_scr[0:1, 0:T], 1e-5, None, ALU.add,
               deps=(e3,))
    b.wait(b.v, e4)
    e5 = b.mark(b.v.reciprocal(r_scr[0:1, 0:T], r_rstd[0:1, 0:T]), "ve")
    e6 = b.act(r_rstd[0:1, 0:T], r_scr[0:1, 0:T], AF.Sqrt, deps=(e5,))
    e7 = b.vstt(r_bt[0:1, 0:T], r_mean[0:1, 0:T], -1.0, r_rstd[0:1, 0:T],
                ALU.mult, ALU.mult, deps=(e6,))
    ev_bcR = bcast_f32(b, ps_big[:, 0:T], onesf, r_rstd, T, deps=(e7, ev_stat))
    ev_bcB = bcast_f32(b, ps_big[:, T:2 * T], onesf, r_bt, T, deps=(e7,))
    b.wait(b.v, Ev("ydma", b.cnt["ydma"]), Ev("gx0", b.cnt["gx0"]),
           Ev("gx1", b.cnt["gx1"]))
    prevh = None
    for kc in range(KC):
        sl = hid[:, kc * T:(kc + 1) * T]
        e1_ = b.vtt(xsq[:, 0:T], sl, ps_big[:, 0:T], ALU.mult,
                    deps=(ev_bcR, prevh))
        e2_ = b.vtt(xsA[:, 0:T], xsq[:, 0:T], ps_big[:, T:2 * T], ALU.add,
                    deps=(e1_, ev_bcB))
        e3_ = b.vts(cols[:, 64 + kc:65 + kc], mcol1(4, kc), 1.0, None,
                    ALU.add, deps=(ev_mc1,))
        e4_ = b.vts(sl, xsA[:, 0:T], cols[:, 64 + kc:65 + kc], mcol1(3, kc),
                    ALU.mult, ALU.add, deps=(e2_, e3_))
        prevh = e4_
        b.wait(b.y, e4_)
        b.dma(b.y, h3_o[kc * 128:(kc + 1) * 128, :], sl)
    ev_h3 = prevh
    b.wait(b.g, ev_h3)
    b.dma(b.g, dbgC[0:128, 3072:4096], hid[:, 0:1024])
    if cut <= 4:
        b.wait(b.y, Ev("ydma", b.cnt["ydma"]))
        b.wait(b.g, Ev("gdma", b.cnt["gdma"]))
        b.close()
        return nc


    ev_sq_rd = None
    for kc in range(KC):
        sl = hid[:, kc * T:(kc + 1) * T]
        ev_sq = b.act(sqb[:, 0:T], sl, AF.Square, deps=(ev_h3, ev_sq_rd))
        b.wait(b.t, ev_sq, ev_h3)
        for tb0 in range(0, T, TB):
            i = b.t.matmul(ps_big[32:33, tb0:tb0 + TB], onescol[:, 0:1],
                           sqb[:, tb0:tb0 + TB], start=(kc == 0),
                           stop=(kc == KC - 1))
        ev_sq_rd = b.mark(i, "mm")
    ev_stat = ev_sq_rd
    e3 = b.vts(r_scr[0:1, 0:T], ps_big[32:33, 0:T], 1.0 / D, None, ALU.mult,
               deps=(ev_stat,))
    e4 = b.vts(r_rstd[0:1, 0:T], r_scr[0:1, 0:T], 1e-6, None, ALU.add,
               deps=(e3,))
    b.wait(b.v, e4)
    e5 = b.mark(b.v.reciprocal(r_scr[0:1, 0:T], r_rstd[0:1, 0:T]), "ve")
    e6 = b.act(r_rstd[0:1, 0:T], r_scr[0:1, 0:T], AF.Sqrt, deps=(e5,))
    ev_bcR = bcast_f32(b, ps_big[:, 0:T], onesf, r_rstd, T, deps=(e6, ev_stat))
    prev = None
    for kc in range(KC):
        e1_ = b.vstt(h2b[:, kc * T:(kc + 1) * T], hid[:, kc * T:(kc + 1) * T],
                     cols[:, 16 + kc:17 + kc], ps_big[:, 0:T], ALU.mult,
                     ALU.mult, deps=(ev_bcR, ev_cols, prev))
        prev = e1_
    ev_hn = prev
    b.wait(b.g, ev_hn)
    b.dma(b.g, dbgC[0:128, 4096:5120], h2b[:, 0:1024])
    if cut <= 5:
        b.wait(b.y, Ev("ydma", b.cnt["ydma"]))
        b.wait(b.g, Ev("gdma", b.cnt["gdma"]))
        b.close()
        return nc


    qsev = [None]
    qacc = [None]

    def mk_ep_qk(nocc):
        def ep(oc, tbi, psd, ev_mm):
            e_c = b.vcp(qat[:, oc * T + tbi * TB: oc * T + tbi * TB + TB],
                        psd[:, 0:TB], deps=(ev_mm,))
            b.wait(b.t, e_c)
            return e_c
        return ep

    def qk_stats(nocc, ev_raw):
        ev_sq_rd = None
        for oc in range(nocc):
            sl = qat[:, oc * T:(oc + 1) * T]
            ev_sq = b.act(sqb[:, 0:T], sl, AF.Square, deps=(ev_raw, ev_sq_rd))
            b.wait(b.t, ev_sq, ev_raw)
            for tb0 in range(0, T, TB):
                i = b.t.matmul(ps_big[32:33, tb0:tb0 + TB], onescol[:, 0:1],
                               sqb[:, tb0:tb0 + TB], start=(oc == 0),
                               stop=(oc == nocc - 1))
            ev_sq_rd = b.mark(i, "mm")
        qsev[0] = ev_sq_rd
        qacc[0] = ev_sq_rd
        return ev_sq_rd

    ev_qraw = gemm(wqT, KC, KC, h2b, T, mk_ep_qk(KC),
                   deps=(ev_hn, Ev("ve", b.cnt["ve"]), Ev("se", b.cnt["se"])))
    qk_stats(KC, ev_qraw)
    e3 = b.vts(r_scr[0:1, 0:T], ps_big[32:33, 0:T], 1.0 / D, None, ALU.mult,
               deps=(qsev[0], qacc[0]))
    e4 = b.vts(r_rstd[0:1, 0:T], r_scr[0:1, 0:T], 1e-6, None, ALU.add,
               deps=(e3,))
    b.wait(b.v, e4)
    e5 = b.mark(b.v.reciprocal(r_scr[0:1, 0:T], r_rstd[0:1, 0:T]), "ve")
    e6 = b.act(r_rstd[0:1, 0:T], r_scr[0:1, 0:T], AF.Sqrt, deps=(e5,))
    ev_bcQ = bcast_f32(b, ps_big[:, 0:T], onesf, r_rstd, T,
                       deps=(e6, qsev[0]))

    b.wait(b.g, Ev("ve", b.cnt["ve"]), Ev("gx1", b.cnt["gx1"]))
    e_z = b.mark(b.g.memset(xsB[64:128, 0:T], 0.0), "gc")

    def qk_apply(nocc, base_col, out_hbm, ev_bc):
        prevq = None
        prev_swap = None
        for oc in range(nocc):
            qsl = qat[:, oc * T:(oc + 1) * T]
            e_r1 = b.vstt(xsA[:, 0:T], qsl, cols[:, base_col + oc:base_col + oc + 1],
                          ps_big[:, 0:T], ALU.mult, ALU.mult,
                          deps=(ev_bc, ev_cols, prevq, prev_swap))
            b.wait(b.y, e_r1)
            e_s1 = b.dma(b.y, xsB[0:32, 0:T], xsA[32:64, 0:T])
            e_s2 = b.dma(b.y, xsB[32:64, 0:T], xsA[0:32, 0:T])
            prev_swap = Ev("ydma", e_s2.val)
            e_q1 = b.vtt(xsq[:, 0:T], xsA[:, 0:T], ropeC[:, 0:T], ALU.mult,
                         deps=(e_r1, *evtab))
            e_q2 = b.vtt(xsB[0:64, 0:T], xsB[0:64, 0:T], ropeS[0:64, 0:T],
                         ALU.mult, deps=(prev_swap, e_z, *evtab))
            e_q3 = b.vtt(qsl, xsq[:, 0:T], xsB[:, 0:T], ALU.add,
                         deps=(e_q1, e_q2))
            prevq = e_q3
            b.wait(b.y, e_q3)
            b.dma(b.y, out_hbm[oc * 128:(oc + 1) * 128, :], qsl)
        return prevq

    ev_qdone = qk_apply(KC, 32, q3_o, ev_bcQ)
    b.wait(b.g, ev_qdone)
    b.dma(b.g, dbgC[0:128, 5120:6144], qat[:, 0:1024])
    if cut <= 6:
        b.wait(b.y, Ev("ydma", b.cnt["ydma"]))
        b.wait(b.g, Ev("gdma", b.cnt["gdma"]))
        b.close()
        return nc


    b.wait(b.v, Ev("ydma", b.cnt["ydma"]),
           Ev("gdma", b.cnt["gdma"]))
    ev_kraw = gemm(wkT, 4, KC, h2b, T, mk_ep_qk(4),
                   deps=(ev_qdone, Ev("ve", b.cnt["ve"]), Ev("se", b.cnt["se"])))
    qk_stats(4, ev_kraw)
    e3 = b.vts(r_scr[0:1, 0:T], ps_big[32:33, 0:T], 1.0 / 512.0, None,
               ALU.mult, deps=(qsev[0], qacc[0]))
    e4 = b.vts(r_rstd[0:1, 0:T], r_scr[0:1, 0:T], 1e-6, None, ALU.add,
               deps=(e3,))
    b.wait(b.v, e4)
    e5 = b.mark(b.v.reciprocal(r_scr[0:1, 0:T], r_rstd[0:1, 0:T]), "ve")
    e6 = b.act(r_rstd[0:1, 0:T], r_scr[0:1, 0:T], AF.Sqrt, deps=(e5,))
    ev_bcK = bcast_f32(b, ps_big[:, 0:T], onesf, r_rstd, T,
                       deps=(e6, qsev[0]))
    ev_kdone = qk_apply(4, 48, k3_o, ev_bcK)
    b.wait(b.g, ev_kdone)
    b.dma(b.g, dbgC[0:128, 6144:7168], qat[:, 0:1024])
    if cut <= 7:
        b.wait(b.y, Ev("ydma", b.cnt["ydma"]))
        b.wait(b.g, Ev("gdma", b.cnt["gdma"]))
        b.close()
        return nc


    for stg in range(2):
        ev_kg = None
        for kg in range(4):
            par = kg % 2
            wsl = wsl0 if par == 0 else wsl1
            b.wait(b.g, *wsl_last[par])
            ev_w = b.dma(b.g, wsl[:, 0:2048].rearrange("p (k f) -> p k f", k=4),
                         wv_r[0, kg * 4:(kg + 1) * 4].rearrange("k p f -> p k f"),
                         sem=f"gw{par}")
            b.wait(b.t, ev_w, ev_hn, ev_kdone)
            for st_i in range(4):
                st = stg * 4 + st_i
                for kk in range(4):
                    kc = kg * 4 + kk
                    i2 = b.t.matmul(ps_big[:, st_i * TB:(st_i + 1) * TB],
                                    h2b[:, kc * T + st * 128: kc * T + st * 128 + 128],
                                    wsl[:, kk * TB:(kk + 1) * TB],
                                    start=(kc == 0), stop=(kc == KC - 1))
            e_mm = b.mark(i2, "mm")
            wsl_last[par] = [e_mm]
            ev_kg = e_mm
        for st_i in range(4):
            st = stg * 4 + st_i
            e_v = b.vcp(sqb[:, 0:TB], ps_big[:, st_i * TB:(st_i + 1) * TB],
                        deps=(ev_kg,))
            b.wait(b.y, e_v)
            e_st = b.dma(b.y, v3_o[st * 128:(st + 1) * 128, 0:512],
                         sqb[:, 0:TB])
            b.wait(b.v, Ev("ydma", e_st.val))
        b.wait(b.t, Ev("ve", b.cnt["ve"]))

    b.wait(b.y, Ev("ydma", b.cnt["ydma"]))
    b.wait(b.g, Ev("gdma", b.cnt["gdma"]))
    b.close()
    return nc



def build_launch_D():
    nc = bass.Bass("TRN2", target_bir_lowering=False)
    b = Bld(nc)

    def din(name, shape, dt=F32):
        return nc.dram_tensor(name, list(shape), dt, kind="ExternalInput").ap()

    def dout(name, shape, dt=F32):
        return nc.dram_tensor(name, list(shape), dt, kind="ExternalOutput").ap()

    x3 = din("x3", [KC * 128, T])
    h3 = din("h3", [KC * 128, T])
    q3 = din("Q3", [KC * 128, T], BF16)
    kf = din("Kf3", [4, 128, S], BF16)
    vt = din("Vt3", [4, 64, 128, 128], BF16)
    masks = din("masks", [8, 128, 1024], BF16)
    woT = din("woT", [KC, KC, 128, 128])
    ln2_col = din("ln2_col", [128, KC])
    router_r = din("router_r", [KC, 128, 8])
    rbias_row = din("rbias_row", [1, 8])
    wgD = din("wgD", [8, 8, KC, 128, 128])
    wuD = din("wuD", [8, 8, KC, 128, 128])
    wdD = din("wdD", [8, 2, 8, 128, 1024])
    mcols1_i = din("mcols1_i", [128, 6 * KC])

    x4_o = dout("x4", [KC * 128, T])
    dbgD = dout("dbgD", [128, 8192])

    qat = b.sb("qat", [128, KC * T], BF16)
    K_sb = b.sb("K_sb", [128, S], BF16)
    V_sb = b.sb("V_sb", [128, S], BF16)
    hid = b.sb("hid", [128, KC * T], F32)
    h2b = b.sb("h2b", [128, KC * T], BF16)
    wsl0 = b.sb("wsl0", [128, 2048], BF16)
    wsl1 = b.sb("wsl1", [128, 2048], BF16)
    xsA = b.sb("xsA", [128, 1024], F32)
    xsB = b.sb("xsB", [128, 1024], F32)
    xsq = b.sb("xsq", [128, 1024], F32)
    sqb = b.sb("sqb", [128, 1024], BF16)
    pbf = b.sb("pbf", [128, 512], BF16)
    ptb = b.sb("ptb", [128, 512], BF16)
    mskc = b.sb("mskc", [128, 1024], BF16)
    rowsf = b.sb("rowsf", [128, 5 * 1024], F32)
    rr = b.sb("rr", [128, 48], F32)
    ident = b.sb("ident", [128, 128], BF16)
    onescol = b.sb("onescol", [128, 1], BF16)
    onesf = b.sb("onesf", [128, 128], F32)
    mcols1 = b.sb("mcols1_sb", [128, 6 * KC], F32)
    ccol = b.sb("ccol", [128, KC], F32)
    biasD = b.sb("biasD", [128, 8], F32)

    r_rstd = rowsf[0:1, 1024:2048]
    r_scr = rowsf[0:1, 3072:4096]
    r_scr2 = rowsf[0:1, 4096:5120]
    crow0 = rowsf[0:1, 4096:5120]
    c_rows = rowsf[0:8, 0:1024]
    c_b = xsA
    prb = xsB

    ps_big = b.ps("ps_big", [128, 2048])
    ps_a = b.ps("ps_a", [128, 512])
    ps_b2 = b.ps("ps_b2", [128, 512])
    ps_t = b.ps("ps_t", [128, 1024], BF16)
    ps_pv = b.ps("ps_pv", [128, 128])

    wsl_last = [[], []]

    ev_id = make_ident(b, ident[:])
    b.g.memset(onescol[:], 1.0)
    b.g.memset(rr[:], 0.0)
    b.mark(b.g.memset(onesf[:], 1.0), "gc")

    ev_mc1 = b.dma(b.g, mcols1[:], mcols1_i)

    def mcol1(vi, kc):
        return mcols1[:, vi * KC + kc: vi * KC + kc + 1]

    ev_q = b.dma(b.g, qat[:].rearrange("p (k f) -> p k f", k=KC),
                 q3.rearrange("(k p) f -> p k f", p=128))

    mev = [None]

    class MaskMgr:
        def __init__(self):
            self.loaded = [None]

        def get(self, m):
            if self.loaded[0] == m:
                return mskc[:, 0:1024], mev[0]
            b.wait(b.g, mev[0], Ev("ve", b.cnt["ve"]))
            e = b.dma(b.g, mskc[:, 0:1024], masks[m])
            mev[0] = e
            self.loaded[0] = m
            return mskc[:, 0:1024], e

    mmgr = MaskMgr()
    identf = hid[:, 0:128]
    ev_idf = make_ident(b, identf)
    ten = dict(K_sb=K_sb, V_sb=V_sb, ps_a=ps_a, ps_b2=ps_b2, ps_t=ps_t,
               ps_pv=ps_pv, ident=ident, rr=rr, pbf=pbf, ptb=ptb,
               identf=identf, nrow=rowsf[0:1, 3584:3712],
               ps_bc=ps_big[:, 0:256], onesf=onesf, ev_idf=ev_idf)

    scl = float(1.0 / np.sqrt(128.0))
    ev_attn = causal_attention_masked(b, ten, 16, 4, scl, qat, kf, vt, mmgr,
                                      qat, deps=(ev_q, ev_id))
    b.wait(b.g, ev_attn)
    b.dma(b.g, dbgD[0:128, 0:2048], qat[:, 0:2048])

    hprev = [[], []]

    def ep_oD(oc, tbi, psd, ev_mm):
        par = (oc * 2 + tbi) % 2
        buf = xsA if par == 0 else xsB
        b.wait(b.g, *hprev[par])
        e_x = b.dma(b.g, buf[:, 0:TB],
                    h3[oc * 128:(oc + 1) * 128, tbi * TB:(tbi + 1) * TB],
                    sem=f"gx{par}")
        e1 = b.vstt(hid[:, oc * T + tbi * TB: oc * T + tbi * TB + TB],
                    psd[:, 0:TB], 1.0, buf[:, 0:TB], ALU.mult, ALU.add,
                    deps=(ev_mm, e_x))
        hprev[par] = [e1]
        return e1

    def gemm(w_hbm, noc, nkc, x_sb, x_cols, epilogue, deps=()):
        ev_ep = None
        for oc in range(noc):
            par = oc % 2
            wsl = wsl0 if par == 0 else wsl1
            b.wait(b.g, *wsl_last[par])
            ev_w = b.dma(b.g, wsl[:, 0:nkc * 128].rearrange("p (k f) -> p k f", k=nkc),
                         w_hbm[oc].rearrange("k p f -> p k f"), sem=f"gw{par}")
            for tbi in range(x_cols // TB):
                psd = ps_a if tbi % 2 == 0 else ps_b2
                b.wait(b.t, ev_w, *deps)
                for kc in range(nkc):
                    i = b.t.matmul(psd[:, 0:TB], wsl[:, kc * 128:(kc + 1) * 128],
                                   x_sb[:, kc * x_cols + tbi * TB: kc * x_cols + tbi * TB + TB],
                                   start=(kc == 0), stop=(kc == nkc - 1))
                ev_mm = b.mark(i, "mm")
                wsl_last[par] = [ev_mm]
                ev_ep = epilogue(oc, tbi, psd, ev_mm)
                b.wait(b.t, ev_ep)
        return ev_ep

    ev_hid = gemm(woT, KC, KC, qat, T, ep_oD,
                  deps=(ev_attn, Ev("gdma", b.cnt["gdma"])))
    b.wait(b.g, ev_hid)
    b.dma(b.g, dbgD[0:128, 2048:3072], hid[:, 0:1024])

    ev_fnw = b.dma(b.g, ccol[:], ln2_col)
    ev_sq_rd = None
    for kc in range(KC):
        sl = hid[:, kc * T:(kc + 1) * T]
        ev_sq = b.act(sqb[:, 0:T], sl, AF.Square, deps=(ev_hid, ev_sq_rd))
        b.wait(b.t, ev_sq, ev_hid)
        for tb0 in range(0, T, TB):
            i = b.t.matmul(ps_big[32:33, tb0:tb0 + TB], onescol[:, 0:1],
                           sqb[:, tb0:tb0 + TB], start=(kc == 0),
                           stop=(kc == KC - 1))
        ev_sq_rd = b.mark(i, "mm")
    ev_stat = ev_sq_rd
    e3 = b.vts(r_scr[0:1, 0:T], ps_big[32:33, 0:T], 1.0 / D, None, ALU.mult,
               deps=(ev_stat,))
    e4 = b.vts(r_rstd[0:1, 0:T], r_scr[0:1, 0:T], 1e-6, None, ALU.add,
               deps=(e3,))
    b.wait(b.v, e4)
    e5 = b.mark(b.v.reciprocal(r_scr[0:1, 0:T], r_rstd[0:1, 0:T]), "ve")
    e6 = b.act(r_rstd[0:1, 0:T], r_scr[0:1, 0:T], AF.Sqrt, deps=(e5,))
    ev_bcR = bcast_f32(b, ps_big[:, 0:T], onesf, r_rstd, T, deps=(e6, ev_stat))
    prev = None
    for kc in range(KC):
        e1_ = b.vstt(h2b[:, kc * T:(kc + 1) * T], hid[:, kc * T:(kc + 1) * T],
                     ccol[:, kc:kc + 1], ps_big[:, 0:T], ALU.mult, ALU.mult,
                     deps=(ev_bcR, ev_fnw, prev))
        prev = e1_
    ev_h2 = prev
    b.wait(b.g, ev_h2)
    b.dma(b.g, dbgD[0:128, 3072:4096], h2b[:, 0:1024])

    ev_gw = b.dma(b.g, wsl0[:, 0:KC * 8].rearrange("p (k f) -> p k f", k=KC),
                  router_r.rearrange("k p f -> p k f"))
    wsl_last[0] = [Ev("mm", b.cnt["mm"])]
    ev_bl = b.dma(b.g, r_scr2[0:1, 0:8], rbias_row)
    b.wait(b.t, ev_bl, ev_h2)
    e_bb = b.mark(b.t.matmul(ps_a[:, 0:8], onesf[0:1, :], r_scr2[0:1, 0:8],
                             start=True, stop=True), "mm")
    e_bc = b.vcp(biasD[:, 0:8], ps_a[:, 0:8], deps=(e_bb,))
    ev_cr_list = []
    prev_tr = None
    for tt_ in range(8):
        b.wait(b.t, ev_gw, ev_h2, prev_tr, e_bc)
        for kc in range(KC):
            i = b.t.matmul(ps_a[0:128, 0:8],
                           h2b[:, kc * T + tt_ * 128: kc * T + tt_ * 128 + 128],
                           wsl0[:, kc * 8:(kc + 1) * 8],
                           start=(kc == 0), stop=(kc == KC - 1))
        e_mm = b.mark(i, "mm")
        c0 = tt_ * 6
        e_sig = b.act(c_b[:, 0:8], ps_a[0:128, 0:8], AF.Sigmoid, deps=(e_mm,))
        e_y = b.vtt(c_b[:, 8:16], c_b[:, 0:8], biasD[:, 0:8], ALU.add,
                    deps=(e_sig, e_bc))
        e_m1 = b.red(prb[:, c0:c0 + 1], c_b[:, 8:16], ALU.max, deps=(e_y,))
        e_eq = b.vts(c_b[:, 16:24], c_b[:, 8:16], prb[:, c0:c0 + 1], None,
                     ALU.is_equal, deps=(e_m1,))
        e_msk = b.vstt(c_b[:, 24:32], c_b[:, 16:24], -1e30, c_b[:, 8:16],
                       ALU.mult, ALU.add, deps=(e_eq,))
        e_m2 = b.red(prb[:, c0 + 1:c0 + 2], c_b[:, 24:32], ALU.max,
                     deps=(e_msk,))
        e_sel = b.vts(c_b[:, 16:24], c_b[:, 8:16], prb[:, c0 + 1:c0 + 2],
                      None, ALU.is_ge, deps=(e_m2,))
        e_w = b.vtt(c_b[:, 24:32], c_b[:, 0:8], c_b[:, 16:24], ALU.mult,
                    deps=(e_sel,))
        e_sw = b.red(prb[:, c0 + 2:c0 + 3], c_b[:, 24:32], ALU.add,
                     deps=(e_w,))
        b.wait(b.v, e_sw)
        e_rd = b.mark(b.v.reciprocal(prb[:, c0 + 3:c0 + 4],
                                     prb[:, c0 + 2:c0 + 3]), "ve")
        e_cmb = b.vts(c_b[:, 32:40], c_b[:, 24:32], prb[:, c0 + 3:c0 + 4],
                      None, ALU.mult, deps=(e_rd,))
        e_cb16 = b.vcp(sqb[:, 0:8], c_b[:, 32:40], deps=(e_cmb,))
        b.wait(b.t, e_cb16, ev_id)
        tp_e = b.tp(ps_t[0:8, 0:128], sqb[:, 0:8], ident[:], markit=True)
        e_cp = b.vcp(c_rows[0:8, tt_ * 128:(tt_ + 1) * 128], ps_t[0:8, 0:128],
                     deps=(tp_e,))
        prev_tr = e_cp
        ev_cr_list.append(e_cp)
    wsl_last[0] = [Ev("mm", b.cnt["mm"])]
    b.wait(b.g, *ev_cr_list)
    b.dma(b.g, dbgD[0:8, 4096:5120], c_rows[0:8, 0:1024])
    b.wait(b.v, Ev("gdma", b.cnt["gdma"]))

    act_v = qat
    ev_moe_prev = None
    for e_ in range(8):
        b.wait(b.y, *ev_cr_list, ev_moe_prev)
        e_c0 = b.dma(b.y, crow0[0:1, :], c_rows[e_:e_ + 1, :])
        b.wait(b.t, Ev("ydma", b.cnt["ydma"]))
        e_cb2 = bcast_f32(b, ps_big[:, 1024:2048], onesf, crow0, 1024)
        e_cbs = b.vcp(c_b[:, 0:1024], ps_big[:, 1024:2048], deps=(e_cb2,))
        ev_act = []
        for ic in range(8):
            b.wait(b.g, *wsl_last[0])
            ev_wg = b.dma(b.g, wsl0[:, 0:KC * 128].rearrange("p (k f) -> p k f", k=KC),
                          wgD[e_, ic].rearrange("k p f -> p k f"), sem="gw0")
            b.wait(b.g, *wsl_last[1])
            ev_wu = b.dma(b.g, wsl1[:, 0:KC * 128].rearrange("p (k f) -> p k f", k=KC),
                          wuD[e_, ic].rearrange("k p f -> p k f"), sem="gw1")
            for tbi in range(2):
                b.wait(b.t, ev_wg, ev_h2, ev_moe_prev)
                for kc in range(KC):
                    i = b.t.matmul(ps_a[:, 0:TB], wsl0[:, kc * 128:(kc + 1) * 128],
                                   h2b[:, kc * T + tbi * TB: kc * T + tbi * TB + TB],
                                   start=(kc == 0), stop=(kc == KC - 1))
                e_g = b.mark(i, "mm")
                b.wait(b.t, ev_wu)
                for kc in range(KC):
                    i = b.t.matmul(ps_b2[:, 0:TB], wsl1[:, kc * 128:(kc + 1) * 128],
                                   h2b[:, kc * T + tbi * TB: kc * T + tbi * TB + TB],
                                   start=(kc == 0), stop=(kc == KC - 1))
                e_u = b.mark(i, "mm")
                e_sg = b.act(sqb[:, 0:TB], ps_a[:, 0:TB], AF.Silu, deps=(e_g,))
                act_sl = act_v[:, ic * T + tbi * TB: ic * T + tbi * TB + TB]
                e_a1 = b.vstt(act_sl, sqb[:, 0:TB], 1.0, ps_b2[:, 0:TB],
                              ALU.mult, ALU.mult, deps=(e_sg, e_u, e_cbs))
                e_a2 = b.vstt(act_sl, act_sl, 1.0,
                              c_b[:, tbi * TB: tbi * TB + TB],
                              ALU.mult, ALU.mult, deps=(e_a1,))
                ev_act.append(e_a2)
                b.wait(b.t, e_sg, e_a1, e_a2)
            wsl_last[0] = [e_g]
            wsl_last[1] = [e_u]
        ev_dn = None
        for och in range(2):
            b.wait(b.g, Ev("mm", b.cnt["mm"]))
            for ic in range(8):
                e_wd = b.dma(b.g, K_sb[:, ic * 1024:(ic + 1) * 1024],
                             wdD[e_, och, ic], sem="gx0")
            wd_ev = Ev("gx0", b.cnt["gx0"])
            for oc_i in range(8):
                oc = och * 8 + oc_i
                for tbi in range(2):
                    psd = ps_a if (oc_i * 2 + tbi) % 2 == 0 else ps_b2
                    b.wait(b.t, wd_ev, *ev_act, ev_dn)
                    for ic in range(8):
                        lh = K_sb[:, ic * 1024 + oc_i * 128: ic * 1024 + oc_i * 128 + 128]
                        i = b.t.matmul(psd[:, 0:TB], lh,
                                       act_v[:, ic * T + tbi * TB: ic * T + tbi * TB + TB],
                                       start=(ic == 0), stop=(ic == 7))
                    e_d = b.mark(i, "mm")
                    e_f = b.vstt(hid[:, oc * T + tbi * TB: oc * T + tbi * TB + TB],
                                 psd[:, 0:TB], 1.0,
                                 hid[:, oc * T + tbi * TB: oc * T + tbi * TB + TB],
                                 ALU.mult, ALU.add, deps=(e_d,))
                    ev_dn = e_f
        ev_moe_prev = ev_dn

    x4prev = [[], []]
    ev_x4 = None
    for kc in range(KC):
        for tbi in range(2):
            par = (kc * 2 + tbi) % 2
            buf = xsA if par == 0 else xsB
            b.wait(b.g, *x4prev[par])
            e_x = b.dma(b.g, buf[:, 0:TB],
                        x3[kc * 128:(kc + 1) * 128, tbi * TB:(tbi + 1) * TB],
                        sem=f"gx{par}")
            e1 = b.vstt(xsq[:, 0:TB],
                        hid[:, kc * T + tbi * TB: kc * T + tbi * TB + TB],
                        mcol1(5, kc), buf[:, 0:TB], ALU.mult, ALU.add,
                        deps=(ev_moe_prev, e_x, ev_mc1, ev_x4))
            x4prev[par] = [e1]
            b.wait(b.y, e1)
            e_st = b.dma(b.y, x4_o[kc * 128:(kc + 1) * 128, tbi * TB:(tbi + 1) * TB],
                         xsq[:, 0:TB])
            ev_x4 = Ev("ydma", e_st.val)

    b.wait(b.y, Ev("ydma", b.cnt["ydma"]))
    b.wait(b.g, Ev("gdma", b.cnt["gdma"]))
    b.close()
    return nc
